# version 01
# baseline (speedup 1.0000x reference)
import math

import numpy as np
import ml_dtypes
from contextlib import ExitStack

import concourse.bass as bass
import concourse.mybir as mybir
from concourse.bass_utils import run_bass_kernel_spmd

F32 = mybir.dt.float32
BF16 = mybir.dt.bfloat16
I32 = mybir.dt.int32
AF = mybir.ActivationFunctionType
ALU = mybir.AluOpType
AX = mybir.AxisListType

import os
PHASE = int(os.environ.get("KPHASE", "99"))
P2LIM = int(os.environ.get("KP2", "99"))
KDBG = int(os.environ.get("KDBG", "0"))
S = 4096
DM = 2048
L_INIT = 0.8 - 0.6 * math.exp(-0.3)
OWN_SB = {0: [0, 1, 6, 7], 1: [2, 3, 4, 5]}


class Sems:
    def __init__(self):
        self.h = {}
        self.c = {}

    def add(self, name, handle):
        self.h[name] = handle
        self.c[name] = 0

    def inc(self, instr, name, n=1):
        instr.then_inc(self.h[name], n)
        self.c[name] += n
        return self.c[name]

    def dma(self, instr, name):
        return self.inc(instr, name, 16)

    def val(self, name):
        return self.c[name]

    def wt(self, eng, name, val=None):
        eng.wait_ge(self.h[name], self.c[name] if val is None else val)

    def snap(self):
        return dict(self.c)

    def restore(self, s):
        self.c = dict(s)


class K:
    pass


class VEGuard:

    _OPS = {"tensor_copy", "tensor_scalar", "tensor_single_scalar",
            "tensor_tensor", "tensor_add", "tensor_sub", "tensor_mul",
            "tensor_max", "reduce_sum", "reduce_max", "tensor_reduce",
            "reciprocal", "scalar_tensor_tensor", "memset", "select",
            "tensor_scalar_add", "tensor_scalar_sub", "tensor_scalar_mul",
            "tensor_scalar_max", "tensor_scalar_min"}

    def __init__(self, ve):
        self._ve = ve

    def __getattr__(self, name):
        f = getattr(self._ve, name)
        if name not in self._OPS:
            return f

        def wrapped(out, *a, **kw):
            ins = f(out, *a, **kw)
            try:
                small = out.free_size() < 64
            except Exception:
                small = True
            if small:
                self._ve.drain()
            return ins
        return wrapped


def psl(k, slot):
    return k.psTr[:, slot * 128:(slot + 1) * 128]


def bcast(k, dst_ap, src_row_ap, n):
    sm, te, ve = k.sm, k.te, k.ve
    ps = k.psC[2 + (k.bc_cnt % 2)]
    if k.bc_hist[k.bc_cnt % 2]:
        te.wait_ge(sm.h["cGP"], k.bc_hist[k.bc_cnt % 2])
    sm.wt(te, "cVE")
    k.bc_te_last = sm.inc(te.matmul(ps[:, 0:n], k.onesr[0:1, :], src_row_ap,
                                    start=True, stop=True), "cTE")
    sm.wt(ve, "cTE")
    done = sm.inc(ve.tensor_copy(dst_ap, ps[:, 0:n]), "cGP")
    k.bc_hist[k.bc_cnt % 2] = done
    k.bc_cnt += 1
    return done


def emit_consts(k):
    sm, g, te, ve, se, sy = k.sm, k.gp, k.te, k.ve, k.se, k.sy
    for out, in_ in [
        (k.rs[:], k.d_role), (k.lva[:], k.d_lva),
        (k.lvb[:], k.d_lvb), (k.trim[:], k.d_tri), (k.identb[:], k.d_id),
    ]:
        sm.dma(g.dma_start(out=out, in_=in_), "dC")
    engines = [te, k.ve_raw, se, g, sy]
    k.idx = []
    for eng in engines:
        sm.wt(eng, "dC")
        r = eng.alloc_register()
        eng.reg_load(r, k.rs[:1, :1])
        k.idx.append(eng.compute_val(r))
    k.sw_engines = engines

    ve.memset(k.ones_b[:], 1.0)
    ve.memset(k.onesr[:], 1.0)
    ve.memset(k.eps5[:], 1e-5)
    ve.memset(k.eps6[:], 1e-6)
    k.onesf_done = sm.inc(ve.memset(k.onesf[:], 1.0), "cVE")
    ve.memset(k.maxq2[:], 0.0)
    ve.memset(k.maxk2[:], 0.0)
    ve.memset(k.wsum[:], 0.0)

    ve.tensor_mul(k.tiny[:1, 0:16], k.lva[:1, 0:16], k.lvb[:1, 0:16])
    sm.inc(ve.reduce_sum(k.tiny2[:1, 0:2],
                         k.tiny[:1, 0:16].rearrange("p (a b) -> p a b", a=2),
                         axis=AX.X), "cVE")
    sm.wt(se, "cVE")
    sm.inc(se.activation(out=k.tiny2[:1, 4:6], in_=k.tiny2[:1, 0:2],
                         func=AF.Exp), "cSE")
    sm.wt(ve, "cSE")
    ve.tensor_sub(k.tiny2[:1, 8:9], k.tiny2[:1, 4:5], k.tiny2[:1, 5:6])
    sm.inc(ve.tensor_scalar(k.tiny2[:1, 8:9], k.tiny2[:1, 8:9], L_INIT,
                            None, ALU.add), "cVE")
    bcast(k, k.lam_pb[:], k.tiny2[:1, 8:9], 1)



def emit_p0(k, p0):
    sm, te, ve, se, g = k.sm, k.te, k.ve, k.se, k.gp

    def sb0(name, shape, dt):
        return p0.enter_context(k.nc.sbuf_tensor(name, shape, dt))

    wmbuf = [sb0(f"wmbuf{i}", [128, 16 * 512], BF16) for i in range(2)]
    ct_row = sb0("ct_row", [1, DM], F32)
    modrow = [sb0(f"modrow{i}", [1, 512], F32) for i in range(2)]
    growS = sb0("growS", [1, 512], F32)
    grow2S = sb0("grow2S", [1, 512], F32)
    wfbuf = sb0("wfbuf0", [128, DM], F32)

    sm.dma(g.dma_start(out=ct_row[:], in_=k.d_cT), "dC")
    sm.wt(te, "dC")
    te.wait_ge(sm.h["cVE"], k.onesf_done)
    t_ins = None
    for i in range(16):
        t_ins = te.transpose(k.psS0[:, i:i + 1],
                             ct_row[:1, i * 128:(i + 1) * 128],
                             k.onesf[0:1, 0:1])
    sm.inc(t_ins, "cTE")
    sm.wt(ve, "cTE")
    k.ct_ev = sm.inc(ve.tensor_copy(k.ct[:], k.psS0[:, 0:16]), "cEV")
    sm.wt(se, "cEV")
    sm.inc(se.activation(out=k.ctb[:], in_=k.ct[:], func=AF.Silu), "cSE")
    k.silu_done = sm.val("cSE")

    slot_free = [0, 0]
    row_free = [0, 0]
    mod_ev = []

    def gate_transform(r, dst_full, off):
        if k.bc_te_last:
            ve.wait_ge(sm.h["cTE"], k.bc_te_last)
        ve.tensor_scalar(growS[:1, :], r, 0.0, None, ALU.is_ge)
        ve.tensor_scalar(growS[:1, :], growS[:1, :], 2.0, -1.0,
                         ALU.mult, ALU.add)
        ve.tensor_mul(grow2S[:1, :], r, growS[:1, :])
        ve.tensor_scalar(grow2S[:1, :], grow2S[:1, :], 0.25, None, ALU.max)
        sm.inc(ve.tensor_mul(growS[:1, :], growS[:1, :], grow2S[:1, :]), "cVE")
        bcast(k, dst_full[:, off * 512:(off + 1) * 512], growS[:1, :], 512)

    for oc in range(24):
        buf = wmbuf[oc % 2]
        if oc >= 2:
            g.wait_ge(sm.h["cTE"], slot_free[oc % 2])
        for hh in range(2):
            sm.dma(g.dma_start(
                out=buf[:, hh * 8 * 512:(hh + 1) * 8 * 512]
                    .rearrange("p (c o) -> p c o", c=8),
                in_=k.v_wmod[:, hh * 8:(hh + 1) * 8, oc * 512:(oc + 1) * 512]),
                "dM")
        sm.wt(te, "dM")
        te.wait_ge(sm.h["cSE"], k.silu_done)
        if oc >= 2:
            te.wait_ge(sm.h["cEV"], mod_ev[oc - 2])
        else:
            te.wait_ge(sm.h["cEV"], k.ct_ev)
        ins = None
        for kc in range(16):
            ins = te.matmul(k.psS0[0:1, 0:512], k.ctb[:, kc:kc + 1],
                            buf[:, kc * 512:(kc + 1) * 512],
                            start=(kc == 0), stop=(kc == 15))
        slot_free[oc % 2] = sm.inc(ins, "cTE")
        sm.wt(ve, "cTE")
        if oc >= 2 and oc - 2 < 8:
            ve.wait_ge(sm.h["cTE"], row_free[oc % 2])
        if oc >= 2 and oc - 2 >= 8:
            ve.wait_ge(sm.h["cTE"], k.bc_te_last)
        mod_ev.append(sm.inc(
            ve.tensor_copy(modrow[oc % 2][:1, :], k.psS0[0:1, 0:512]), "cEV"))
        r = modrow[oc % 2][:1, :]
        if oc < 8:
            sm.wt(te, "cEV")
            t_ins = None
            for i in range(4):
                t_ins = te.transpose(k.psS1[:, oc * 4 + i:oc * 4 + i + 1],
                                     r[:, i * 128:(i + 1) * 128],
                                     k.onesf[0:1, 0:1])
            row_free[oc % 2] = sm.inc(t_ins, "cTE")
            if oc == 7:
                sm.wt(ve, "cTE")
                sm.inc(ve.tensor_copy(k.modT[:], k.psS1[:, 0:32]), "cEV")
                sm.inc(ve.tensor_scalar(k.ca1v[:], k.modT[:, 16:32], 1.0,
                                        None, ALU.add), "cVE")
                k.modT_done = sm.val("cVE")
        elif oc < 12:
            gate_transform(r, k.ga_full, oc - 8)
        elif oc < 16:
            bcast(k, k.sb_full[:, (oc - 12) * 512:(oc - 11) * 512],
                  r, 512)
        elif oc < 20:
            if k.bc_te_last:
                ve.wait_ge(sm.h["cTE"], k.bc_te_last)
            sm.inc(ve.tensor_scalar(growS[:1, :], r, 1.0, None, ALU.add),
                   "cVE")
            bcast(k, k.cb1_full[:, (oc - 16) * 512:(oc - 15) * 512],
                  growS[:1, :], 512)
        else:
            gate_transform(r, k.gb_full, oc - 20)
    k.vecs_done_gp = sm.val("cGP")

    wf_ve = 0
    for sl in range(16):
        if sl >= 1:
            g.wait_ge(sm.h["cVE"], wf_ve)
        sm.dma(g.dma_start(out=wfbuf[:], in_=k.v_wfT[:, sl:sl + 1, :]), "dF")
        sm.wt(ve, "dF")
        ve.reduce_sum(k.wfred[:], wfbuf[:], axis=AX.X,
                      apply_absolute_value=True)
        wf_ve = sm.inc(ve.tensor_add(k.wsum[:], k.wsum[:], k.wfred[:]), "cVE")
    sm.wt(te, "cVE")
    sm.wt(te, "cEV")
    sm.inc(te.matmul(k.psS1[0:1, 0:1], k.wsum[:], k.onesf[:],
                     start=True, stop=True), "cTE")
    sm.wt(ve, "cTE")
    ve.tensor_scalar(k.tiny[:1, 0:1], k.psS1[0:1, 0:1],
                     1.0 / float(DM * DM), 1e-6, ALU.mult, ALU.max)
    sm.inc(ve.reciprocal(k.tiny[:1, 1:2], k.tiny[:1, 0:1]), "cVE")
    bcast(k, k.ws_pb[:], k.tiny[:1, 0:1], 1)
    bcast(k, k.rws_pb[:], k.tiny[:1, 1:2], 1)
    k.ws_done = sm.val("cGP")


def emit_p1a(k, p1):
    sm, te, ve, se, sy = k.sm, k.te, k.ve, k.se, k.sy

    def sb1(name, shape, dt):
        return p1a_stack.enter_context(k.nc.sbuf_tensor(name, shape, dt))

    p1a_stack = ExitStack()
    xt = [sb1(f"xt{i}", [128, DM], F32) for i in range(2)]
    hts = [sb1(f"hts{i}", [128, 16 * 128], BF16) for i in range(2)]
    s_sum = sb1("s_sum", [128, 1], F32)
    s_mu = sb1("s_mu", [128, 1], F32)
    s_mu2 = sb1("s_mu2", [128, 1], F32)
    s_sqa = sb1("s_sqa", [128, 4], F32)
    s_ssq = sb1("s_ssq", [128, 1], F32)
    s_var = sb1("s_var", [128, 1], F32)
    s_sd = sb1("s_sd", [128, 1], F32)
    s_rstd = sb1("s_rstd", [128, 1], F32)
    xnb2 = [sb1(f"xnb{i}", [128, DM], BF16) for i in range(2)]

    norm_done = []
    hts_spill = []
    tr_evac = []
    xnb_tr = []

    sy.wait_ge(sm.h["cGP"], k.ws_done)
    sm.dma(sy.dma_start(out=xt[0][:], in_=k.v_x[:, 0:1, :]), "dX")
    sm.dma(sy.dma_start(out=xt[1][:], in_=k.v_x[:, 1:2, :]), "dX")
    for tt in range(32):
        b = xt[tt % 2]
        ve.wait_ge(sm.h["dX"], 16 * (tt + 1))
        ve.reduce_sum(s_sum[:], b[:], axis=AX.X)
        se.wait_ge(sm.h["dX"], 16 * (tt + 1))
        if tt >= 1:
            se.wait_ge(sm.h["cVE"], norm_done[tt - 1])
        ins = None
        for i in range(4):
            ins = se.activation(out=k.psC[i][:, 0:512],
                                in_=b[:, i * 512:(i + 1) * 512],
                                func=AF.Square, accum_out=s_sqa[:, i:i + 1])
        sq_done = sm.inc(ins, "cSE")
        ve.tensor_scalar(s_mu[:], s_sum[:], 1.0 / DM, None, ALU.mult)
        ve.tensor_scalar(s_mu2[:], s_mu[:], s_mu[:, 0:1], None, ALU.mult)
        ve.wait_ge(sm.h["cSE"], sq_done)
        ve.reduce_sum(s_ssq[:], s_sqa[:], axis=AX.X)
        ve.tensor_scalar(s_var[:], s_ssq[:], 1.0 / DM, None, ALU.mult)
        sm.inc(ve.tensor_sub(s_var[:], s_var[:], s_mu2[:]), "cVE")
        sm.wt(se, "cVE")
        sd_done = sm.inc(se.activation(out=s_sd[:], in_=s_var[:],
                                       func=AF.Sqrt, bias=k.eps5[:, 0:1]), "cSE")
        ve.wait_ge(sm.h["cSE"], sd_done)
        ve.reciprocal(s_rstd[:], s_sd[:])
        if KDBG and tt == 0:
            ve.tensor_copy(k.dbgst[:, 0:1], s_sum[:])
            ve.tensor_copy(k.dbgst[:, 1:2], s_mu[:])
            ve.tensor_copy(k.dbgst[:, 2:3], s_mu2[:])
            ve.tensor_copy(k.dbgst[:, 3:4], s_ssq[:])
            ve.tensor_copy(k.dbgst[:, 4:5], s_var[:])
            ve.tensor_copy(k.dbgst[:, 5:6], s_sd[:])
            ve.tensor_copy(k.dbgst[:, 6:7], s_rstd[:])
            ve.tensor_copy(k.dbgst[:, 7:11], s_sqa[:])
        xnb = xnb2[tt % 2]
        if tt >= 2:
            ve.wait_ge(sm.h["cTE"], xnb_tr[tt - 2])
        nrm = sm.inc(ve.tensor_scalar(xnb[:], b[:], s_mu[:, 0:1],
                                      s_rstd[:, 0:1], ALU.subtract, ALU.mult),
                     "cVE")
        norm_done.append(nrm)
        if tt + 2 < 32:
            sy.wait_ge(sm.h["cVE"], nrm)
            sy.wait_ge(sm.h["cSE"], sq_done)
            sm.dma(sy.dma_start(out=xt[tt % 2][:],
                                in_=k.v_x[:, tt + 2:tt + 3, :]), "dX")
        hs = hts[tt % 2]
        for half in range(2):
            if len(tr_evac) >= 2:
                te.wait_ge(sm.h["cEV"], tr_evac[-2])
            te.wait_ge(sm.h["cVE"], nrm)
            ins = None
            for i in range(8):
                dc = half * 8 + i
                ins = te.transpose(psl(k, dc), xnb[:, dc * 128:(dc + 1) * 128],
                                   k.identb[:])
            tdone = sm.inc(ins, "cTE")
            if half == 1:
                xnb_tr.append(tdone)
            ve.wait_ge(sm.h["cTE"], tdone)
            if tt >= 2 and half == 0:
                ve.wait_ge(sm.h["dSP"], hts_spill[tt - 2])
            if tt == 0 and half == 0:
                ve.wait_ge(sm.h["cVE"], k.modT_done)
            ev = None
            for i in range(8):
                dc = half * 8 + i
                ev = ve.tensor_scalar(hs[:, dc * 128:(dc + 1) * 128],
                                      psl(k, dc),
                                      k.ca1v[:, dc:dc + 1], k.modT[:, dc:dc + 1],
                                      ALU.mult, ALU.add)
            tr_evac.append(sm.inc(ev, "cEV"))
        se.wait_ge(sm.h["cEV"], tr_evac[-1])
        hts_spill.append(sm.dma(
            se.dma_start(out=k.v_hTw[:, :, tt * 128:(tt + 1) * 128],
                         in_=hs[:].rearrange("p (c w) -> p c w", c=16)),
            "dSP"))
    k.p1a_spill = sm.val("dSP")
    p1a_stack.close()


def emit_proj_pass(k, wview, m0, m1, blocks, spill_view, norm_dst):
    sm, te, ve, se, sy, g = k.sm, k.te, k.ve, k.se, k.sy, k.gp
    wres, htsl, stg, sqt = k.wres, k.htsl, k.stg, k.sqt

    nm = m1 - m0
    owidth = nm * 128
    g.wait_ge(sm.h["cPJ"], sm.val("cPJ"))
    g.wait_ge(sm.h["dSP"], k.p1a_spill)
    for kc in range(16):
        sm.dma(g.dma_start(out=wres[:, kc * 2048:kc * 2048 + owidth],
                           in_=wview[:, kc:kc + 1, m0 * 128:m1 * 128]), "dW")
    w_loaded = sm.val("dW")
    psrot = [k.psS0, k.psS1]
    ev_of_ps = [0, 0]
    spill_of_stg = [0, 0, 0, 0]
    dh_block = []
    pj_block_end = []
    pscnt = 0
    nrm_hist = []
    nm_prev = sm.val("cNM")
    for ib, (nb, scol) in enumerate(blocks):
        slot = htsl[ib % 2]
        if ib >= 2:
            sy.wait_ge(sm.h["cPJ"], pj_block_end[ib - 2])
        if ib == 0:
            sy.wait_ge(sm.h["dSP"], k.p1a_spill)
            sy.wait_ge(sm.h["cPJ"], sm.val("cPJ"))
        elif ib == 1:
            sy.wait_ge(sm.h["cPJ"], sm.val("cPJ") - nm)
        for hh in range(2):
            sm.dma(sy.dma_start(
                out=slot[:, hh * 8 * 512:(hh + 1) * 8 * 512]
                    .rearrange("p (c w) -> p c w", c=8),
                in_=k.v_hTw[:, hh * 8:(hh + 1) * 8,
                            nb * 512:(nb + 1) * 512]), "dH")
        dh_block.append(sm.val("dH"))
        for mp in range(nm // 2):
            for mo in (0, 1):
                m = m0 + 2 * mp + mo
                ps = psrot[pscnt % 2]
                if pscnt == 0:
                    te.wait_ge(sm.h["dW"], w_loaded)
                te.wait_ge(sm.h["dH"], dh_block[ib])
                if ev_of_ps[pscnt % 2]:
                    te.wait_ge(sm.h["cEV"], ev_of_ps[pscnt % 2])
                ins = None
                lm = m - m0
                for kc in range(16):
                    ins = te.matmul(
                        ps[:, 0:512],
                        wres[:, kc * 2048 + lm * 128:kc * 2048 + (lm + 1) * 128],
                        slot[:, kc * 512:(kc + 1) * 512],
                        start=(kc == 0), stop=(kc == 15))
                pj = sm.inc(ins, "cPJ")
                sg = stg[pscnt % 4]
                ve.wait_ge(sm.h["cPJ"], pj)
                if spill_of_stg[pscnt % 4]:
                    ve.wait_ge(sm.h["dSP"], spill_of_stg[pscnt % 4])
                ev_of_ps[pscnt % 2] = sm.inc(
                    ve.tensor_copy(sg[:], ps[:, 0:512]), "cEV")
                se.wait_ge(sm.h["cEV"], ev_of_ps[pscnt % 2])
                spill_of_stg[pscnt % 4] = sm.dma(
                    se.dma_start(out=spill_view[:, m:m + 1,
                                               scol * 512:(scol + 1) * 512],
                                 in_=sg[:]), "dSP")
                pscnt += 1
            if norm_dst is not None:
                hd = m0 // 2 + mp
                n = len(nrm_hist)
                psn = k.psC[n % 2]
                sg_e = stg[(pscnt - 2) % 4]
                sg_o = stg[(pscnt - 1) % 4]
                ve.wait_ge(sm.h["cEV"], ev_of_ps[(pscnt - 1) % 2])
                ve.wait_ge(sm.h["cNM"], nrm_hist[n - 2] if n >= 2 else nm_prev)
                ve.wait_ge(sm.h["cQ2"], sm.val("cQ2"))
                ve.tensor_mul(sqt[0][:], sg_e[:], sg_e[:])
                sq = sm.inc(ve.tensor_mul(sqt[1][:], sg_o[:], sg_o[:]), "cVE")
                te.wait_ge(sm.h["cVE"], sq)
                te.wait_ge(sm.h["cNM"], nrm_hist[n - 2] if n >= 2 else nm_prev)
                te.matmul(psn[0:1, 0:512], k.ones_b[:], sqt[0][:],
                          start=True, stop=False)
                q2 = sm.inc(te.matmul(psn[0:1, 0:512], k.ones_b[:], sqt[1][:],
                                      start=False, stop=True), "cQ2")
                ve.wait_ge(sm.h["cQ2"], q2)
                ve.reduce_max(k.tiny2[:1, 2:3], psn[0:1, 0:512], axis=AX.X)
                nrm_hist.append(sm.inc(
                    ve.tensor_max(norm_dst[:1, hd:hd + 1],
                                  norm_dst[:1, hd:hd + 1],
                                  k.tiny2[:1, 2:3]), "cNM"))
        pj_block_end.append(sm.val("cPJ"))


def emit_v_pass(k):
    sm, te, ve, se, sy, g = k.sm, k.te, k.ve, k.se, k.sy, k.gp
    wres, htsl, stg, vts = k.wres, k.htsl, k.stg, k.vts

    g.wait_ge(sm.h["cPJ"], sm.val("cPJ"))
    for kc in range(16):
        sm.dma(g.dma_start(out=wres[:, kc * 2048:kc * 2048 + DM],
                           in_=k.v_wvT[:, kc:kc + 1, :]), "dW")
    w_loaded = sm.val("dW")

    psrot = [k.psS0, k.psS1]
    ev_of_ps = [0, 0]
    trev_of_half = [0, 0]
    spill_of_vts = [0, 0]
    tr_hist = []
    dh_block = []
    pj_block_end = []
    pscnt = 0
    for nb in range(8):
        slot = htsl[nb % 2]
        if nb >= 2:
            sy.wait_ge(sm.h["cPJ"], pj_block_end[nb - 2])
        elif nb == 0:
            sy.wait_ge(sm.h["cPJ"], sm.val("cPJ"))
        elif nb == 1:
            sy.wait_ge(sm.h["cPJ"], sm.val("cPJ") - 16)
        for hh in range(2):
            sm.dma(sy.dma_start(
                out=slot[:, hh * 8 * 512:(hh + 1) * 8 * 512]
                    .rearrange("p (c w) -> p c w", c=8),
                in_=k.v_hTw[:, hh * 8:(hh + 1) * 8,
                            nb * 512:(nb + 1) * 512]), "dH")
        dh_block.append(sm.val("dH"))
        for m in range(16):
            ps = psrot[pscnt % 2]
            if pscnt == 0:
                te.wait_ge(sm.h["dW"], w_loaded)
            te.wait_ge(sm.h["dH"], dh_block[nb])
            if ev_of_ps[pscnt % 2]:
                te.wait_ge(sm.h["cEV"], ev_of_ps[pscnt % 2])
            ins = None
            for kc in range(16):
                ins = te.matmul(
                    ps[:, 0:512],
                    wres[:, kc * 2048 + m * 128:kc * 2048 + (m + 1) * 128],
                    slot[:, kc * 512:(kc + 1) * 512],
                    start=(kc == 0), stop=(kc == 15))
            pj = sm.inc(ins, "cPJ")
            sg = stg[pscnt % 4]
            ve.wait_ge(sm.h["cPJ"], pj)
            if len(tr_hist) >= 4:
                ve.wait_ge(sm.h["cTE"], tr_hist[-4])
            ev_of_ps[pscnt % 2] = sm.inc(ve.tensor_copy(sg[:], ps[:, 0:512]),
                                         "cEV")
            half = pscnt % 2
            te.wait_ge(sm.h["cEV"], ev_of_ps[pscnt % 2])
            if trev_of_half[half]:
                te.wait_ge(sm.h["cEV"], trev_of_half[half])
            ins = None
            for i in range(4):
                ins = te.transpose(psl(k, half * 8 + i),
                                   sg[:, i * 128:(i + 1) * 128], k.identb[:])
            tr = sm.inc(ins, "cTE")
            tr_hist.append(tr)
            vb = vts[pscnt % 2]
            ve.wait_ge(sm.h["cTE"], tr)
            if spill_of_vts[pscnt % 2]:
                ve.wait_ge(sm.h["dSP"], spill_of_vts[pscnt % 2])
            ev = None
            for i in range(4):
                ev = ve.tensor_copy(vb[:, i * 128:(i + 1) * 128],
                                    psl(k, half * 8 + i))
            trev_of_half[half] = sm.inc(ev, "cEV")
            se.wait_ge(sm.h["cEV"], trev_of_half[half])
            spill_of_vts[pscnt % 2] = sm.dma(
                se.dma_start(out=k.v_vS[:, nb * 4:(nb + 1) * 4,
                                        m * 128:(m + 1) * 128],
                             in_=vb[:].rearrange("p (t w) -> p t w", t=4)),
                "dSP")
            pscnt += 1
        pj_block_end.append(sm.val("cPJ"))


def emit_wfq(k):
    sm, ve, se, g = k.sm, k.ve, k.se, k.gp
    wfbuf2, qv, qi, qf, qe, qt1, wqstage = (
        k.wfbuf2, k.qv, k.qi, k.qf, k.qe, k.qt1, k.wqstage)

    spill_prev = 0
    for sl2 in range(32):
        sl, hh = sl2 // 2, sl2 % 2
        if sl2 >= 1:
            g.wait_ge(sm.h["cQN"], sm.val("cQN"))
        if sl2 == 0:
            g.wait_ge(sm.h["cGP"], k.ws_done)
        sm.dma(g.dma_start(out=wfbuf2[:],
                           in_=k.v_wfT[:, sl:sl + 1, hh * 1024:(hh + 1) * 1024]),
               "dF")
        for st in range(2):
            src = wfbuf2[:, st * 512:(st + 1) * 512]
            sm.wt(ve, "dF")
            if spill_prev and st == 0 and hh == 0:
                ve.wait_ge(sm.h["dSP"], spill_prev)
            ve.tensor_scalar(qv[:], src, k.rws_pb[:, 0:1], None, ALU.mult)
            ve.tensor_copy(qi[:], qv[:])
            ve.tensor_copy(qf[:], qi[:])
            ve.tensor_sub(qe[:], qv[:], qf[:])
            ve.scalar_tensor_tensor(qt1[:], qe[:], 0.5, qf[:],
                                    ALU.is_gt, ALU.add)
            ve.scalar_tensor_tensor(qt1[:], qe[:], -0.5, qt1[:],
                                    ALU.is_ge, ALU.add)
            ve.tensor_scalar(qt1[:], qt1[:], -1.0, 1.0, ALU.add, ALU.min)
            sm.inc(ve.tensor_scalar(
                wqstage[:, (hh * 2 + st) * 512:(hh * 2 + st + 1) * 512],
                qt1[:], -1.0, None, ALU.max), "cQN")
        if hh == 1:
            sm.wt(se, "cQN")
            spill_prev = sm.dma(se.dma_start(out=k.v_wfq[:, sl:sl + 1, :],
                                             in_=wqstage[:]), "dSP")


def emit_negm(k):
    sm, ve, se, g = k.sm, k.ve, k.se, k.gp
    sm.wt(se, "cNM")
    se.activation(out=k.tiny[:1, 0:16], in_=k.maxq2[:1, 0:16], func=AF.Sqrt)
    sq = sm.inc(se.activation(out=k.tiny2[:1, 0:16], in_=k.maxk2[:1, 0:16],
                              func=AF.Sqrt), "cSE")
    ve.wait_ge(sm.h["cSE"], sq)
    ve.tensor_mul(k.tiny[:1, 0:16], k.tiny[:1, 0:16], k.tiny2[:1, 0:16])
    sm.inc(ve.tensor_scalar(k.tiny[:1, 0:16], k.tiny[:1, 0:16],
                            -1.001 / 16.0, None, ALU.mult), "cVE")
    bcast(k, k.negm_pb[:], k.tiny[:1, 0:16], 16)
    k.negm_done = sm.val("cGP")


def emit_debug_dumps(k):
    sm, g, ve = k.sm, k.gp, k.ve
    for eng in [g, ve]:
        eng.wait_ge(sm.h["dSP"], k.all_spills)
        eng.wait_ge(sm.h["cGP"], k.negm_done)
    dv = k.d_dbg_vec
    stg = k.dbgstage
    for dst0, src in [
        (0, k.modT[:]), (32, k.ca1v[:]), (48, k.negm_pb[:]),
        (64, k.lam_pb[:]), (65, k.ws_pb[:]), (66, k.rws_pb[:]),
        (100, k.dbgst[:]),
    ]:
        w = src.shape[-1]
        ve.tensor_copy(stg[:, dst0:dst0 + w], src)
    ve.tensor_copy(stg[0:1, 67:83], k.maxq2[:1, :])
    sm.inc(ve.tensor_copy(stg[0:1, 83:99], k.maxk2[:1, :]), "cVE")
    sm.wt(g, "cVE")
    sm.dma(g.dma_start(out=dv[:, 0:128], in_=stg[:]), "dO")
    for out_sl, src in [
        (dv[:, 128:128 + DM], k.ga_full[:]),
        (dv[:, 128 + DM:128 + 2 * DM], k.gb_full[:]),
        (dv[:, 128 + 2 * DM:128 + 3 * DM], k.cb1_full[:]),
        (dv[:, 128 + 3 * DM:128 + 4 * DM], k.sb_full[:]),
    ]:
        sm.dma(g.dma_start(out=out_sl, in_=src), "dO")
    sm.dma(g.dma_start(out=k.d_dbg_hT, in_=k.hT_all), "dO")
    sm.dma(g.dma_start(out=k.d_dbg_qT, in_=k.qT_all), "dO")
    sm.dma(g.dma_start(out=k.d_dbg_kT, in_=k.kT_all), "dO")
    sm.dma(g.dma_start(out=k.d_dbg_vS, in_=k.vS_all), "dO")
    sm.dma(g.dma_start(out=k.d_dbg_wfq, in_=k.wfq_all), "dO")
    sm.wt(g, "dO")


def emit_attention_sb(k, il, sb):
    sm, te, ve, se, sy = k.sm, k.te, k.ve, k.se, k.sy
    KC = 4 * (sb + 1)
    NG = (KC + 7) // 8

    for hp in range(8):
        if k.av_prev:
            sy.wait_ge(sm.h["cAV"], k.av_prev)
        for c0 in range(0, KC, 8):
            c1 = min(c0 + 8, KC)
            sm.dma(sy.dma_start(
                out=k.vsb[:].rearrange("p (c w) -> p c w", w=257)
                    [:, c0:c1, 0:256],
                in_=k.v_vSr[:, c0:c1, hp * 256:(hp + 1) * 256]), "dV")
        vv_loaded = sm.val("dV")
        for g2 in range(2):
            sh = hp + 8 * g2
            sm.dma(sy.dma_start(
                out=k.qslab[:, g2 * 1024:(g2 + 1) * 1024]
                    .rearrange("p (c w) -> p c w", c=2),
                in_=k.v_qT[:, 2 * sh:2 * sh + 2, il * 512:(il + 1) * 512]),
                "dQ")
        qv_loaded = sm.val("dQ")

        for g2 in range(2):
            sh = hp + 8 * g2
            first_head = (g2 == 0)
            kg_loaded = []
            sc_group_end = [0] * NG
            exp_done = [0] * KC
            msk_done = [0] * KC
            av_of_kc = [0] * KC

            def load_group(kg):
                L = min(1024, KC * 128 - kg * 1024)
                if kg >= 2:
                    sy.wait_ge(sm.h["cSC"], sc_group_end[kg - 2])
                elif k.kbuf_guard[kg % 2]:
                    sy.wait_ge(sm.h["cSC"], k.kbuf_guard[kg % 2])
                sm.dma(sy.dma_start(
                    out=k.kbuf[kg % 2][:]
                        .rearrange("p (c w) -> p c w", c=2)[:, :, 0:L],
                    in_=k.v_kT[:, 2 * sh:2 * sh + 2,
                               kg * 1024:kg * 1024 + L]), "dK")
                kg_loaded.append(sm.val("dK"))

            load_group(0)
            if NG > 1:
                load_group(1)

            def emit_S(kc):
                kg = kc // 8
                j0 = max(0, kc - 4 * sb)
                qoff = j0 * 128
                N = 512 - qoff
                ps = [k.psS0, k.psS1][kc % 2]
                te.wait_ge(sm.h["dK"], kg_loaded[kg])
                if kc == 0:
                    te.wait_ge(sm.h["dQ"], qv_loaded)
                if kc >= 2:
                    te.wait_ge(sm.h["cAX"], exp_done[kc - 2])
                else:
                    if k.ps_guard[kc % 2]:
                        te.wait_ge(sm.h["cAX"], k.ps_guard[kc % 2])
                    if k.tailps_guard[kc % 2]:
                        te.wait_ge(sm.h["cVE"], k.tailps_guard[kc % 2])
                ins = None
                for ch in range(2):
                    ins = te.matmul(
                        ps[:, qoff:qoff + N],
                        k.kbuf[kg % 2][:, ch * 1024 + (kc % 8) * 128:
                                       ch * 1024 + (kc % 8 + 1) * 128],
                        k.qslab[:, g2 * 1024 + ch * 512 + qoff:
                                g2 * 1024 + ch * 512 + qoff + N],
                        start=(ch == 0), stop=(ch == 1))
                sc = sm.inc(ins, "cSC")
                sc_group_end[kg] = sc
                se.wait_ge(sm.h["cSC"], sc)
                ab_guard = (av_of_kc[kc - 2] if kc >= 2
                            else k.ab_guard[kc % 2])
                if ab_guard:
                    se.wait_ge(sm.h["cAV"], ab_guard)
                exp_done[kc] = sm.inc(se.activation(
                    out=k.abuf[kc % 2][:, qoff:qoff + N],
                    in_=ps[:, qoff:qoff + N], func=AF.Exp, scale=1.0 / 16.0,
                    bias=k.negm_pb[:, sh:sh + 1]), "cAX")
                if kc >= 4 * sb:
                    j = kc - 4 * sb
                    ve.wait_ge(sm.h["cAX"], exp_done[kc])
                    msk_done[kc] = sm.inc(ve.tensor_mul(
                        k.abuf[kc % 2][:, j * 128:(j + 1) * 128],
                        k.abuf[kc % 2][:, j * 128:(j + 1) * 128],
                        k.trim[:]), "cMK")
                if kc % 8 == 7 and (kg + 2) < NG:
                    load_group(kg + 2)

            def emit_attnV(kc):
                j0 = max(0, kc - 4 * sb)
                if kc >= 4 * sb:
                    te.wait_ge(sm.h["cMK"], msk_done[kc])
                else:
                    te.wait_ge(sm.h["cAX"], exp_done[kc])
                if kc == 0:
                    te.wait_ge(sm.h["dV"], vv_loaded)
                    if first_head:
                        if k.ep_prev:
                            te.wait_ge(sm.h["cEP"], k.ep_prev)
                        te.wait_ge(sm.h["cVE"], k.vsb_ones)
                    else:
                        te.wait_ge(sm.h["cCE"], k.ce_this)
                ins = None
                for j in range(j0, 4):
                    ins = te.matmul(
                        k.psC[j][:, 0:257],
                        k.abuf[kc % 2][:, j * 128:(j + 1) * 128],
                        k.vsb[:, kc * 257:kc * 257 + 257],
                        start=(kc == 0), stop=(kc == 4 * sb + j))
                av_of_kc[kc] = sm.inc(ins, "cAV")

            for kc in range(KC):
                emit_S(kc)
                if kc >= 1:
                    emit_attnV(kc - 1)
            emit_attnV(KC - 1)

            k.ps_guard[(KC - 1) % 2] = exp_done[KC - 1]
            if KC >= 2:
                k.ps_guard[KC % 2] = exp_done[KC - 2]
            k.ab_guard[(KC - 1) % 2] = av_of_kc[KC - 1]
            if KC >= 2:
                k.ab_guard[KC % 2] = av_of_kc[KC - 2]
            k.kbuf_guard[(NG - 1) % 2] = sc_group_end[NG - 1]
            if NG >= 2:
                k.kbuf_guard[NG % 2] = sc_group_end[NG - 2]
            k.av_last = av_of_kc[KC - 1]

            if first_head:
                ve.wait_ge(sm.h["cAV"], k.av_last)
                ev = None
                for j in range(4):
                    ev = ve.tensor_copy(k.Csb[:, j * 257:(j + 1) * 257],
                                        k.psC[j][:, 0:257])
                k.ce_this = sm.inc(ev, "cCE")

        ve.wait_ge(sm.h["cAV"], k.av_last)
        for j in range(4):
            c0 = k.Csb[:, j * 257:(j + 1) * 257]
            ve.reciprocal(k.rc0[:], c0[:, 256:257])
            ve.reciprocal(k.rc1[:], k.psC[j][:, 256:257])
            sm.inc(ve.tensor_scalar(k.t1f[:], k.psC[j][:, 0:256],
                                    k.rc1[:, 0:1], k.lam_pb[:, 0:1],
                                    ALU.mult, ALU.mult), "cEP")
            ve.scalar_tensor_tensor(k.off[:], c0[:, 0:256], k.rc0[:, 0:1],
                                    k.t1f[:], ALU.mult, ALU.subtract)
            ve.tensor_mul(k.sqf[:], k.off[:], k.off[:])
            grp = sm.inc(ve.reduce_sum(
                k.grp8[:], k.sqf[:].rearrange("p (a b) -> p a b", a=8),
                axis=AX.X), "cVE")
            se.wait_ge(sm.h["cVE"], grp)
            sdq = sm.inc(se.activation(out=k.grs[:], in_=k.grp8[:],
                                       func=AF.Sqrt, scale=1.0 / 32.0,
                                       bias=k.eps6[:, 0:1]), "cSE")
            ve.wait_ge(sm.h["cSE"], sdq)
            ve.reciprocal(k.grr[:], k.grs[:])
            sm.inc(ve.scalar_tensor_tensor(
                k.o_sb[:, j * DM + hp * 256:j * DM + (hp + 1) * 256]
                    .rearrange("p (a b) -> p a b", a=8),
                k.off[:].rearrange("p (a b) -> p a b", a=8),
                (1.0 - L_INIT),
                k.grr[:].rearrange("p (a b) -> p a b", b=1)
                    .broadcast_to([128, 8, 32]),
                ALU.mult, ALU.mult), "cVE")
        k.ep_prev = sm.val("cEP")
        k.av_prev = k.av_last
    k.osb_done = sm.val("cVE")


def emit_tail_sb(k, il, sb):
    sm, te, ve, se, sy, g = k.sm, k.te, k.ve, k.se, k.sy, k.gp

    tr_prev = [0, 0]
    for j in range(4):
        for hb in range(2):
            te.wait_ge(sm.h["cVE"], k.osb_done)
            if tr_prev[hb]:
                te.wait_ge(sm.h["cEV"], tr_prev[hb])
            elif k.psTr_guard[hb]:
                te.wait_ge(sm.h["cEV"], k.psTr_guard[hb])
            ins = None
            for i in range(8):
                oc = hb * 8 + i
                ins = te.transpose(psl(k, hb * 8 + i),
                                   k.o_sb[:, j * DM + oc * 128:
                                          j * DM + (oc + 1) * 128],
                                   k.identb[:])
            tr = sm.inc(ins, "cTE")
            ve.wait_ge(sm.h["cTE"], tr)
            tr_prev[hb] = sm.inc(ve.tensor_copy(
                k.oT_sb[:].rearrange("p (c w) -> p c w", w=512)
                    [:, hb * 8:(hb + 1) * 8, j * 128:(j + 1) * 128],
                k.psTr[:, hb * 1024:hb * 1024 + 1024]
                    .rearrange("p (c w) -> p c w", c=8)), "cEV")
    oT_done = sm.val("cEV")
    k.psTr_guard = list(tr_prev)

    ps_guard = [0, 0]
    xs_guard = [0, 0]
    pscnt = 0
    for mb in range(4):
        if k.wbuf_guard:
            g.wait_ge(sm.h["cTL"], k.wbuf_guard)
        if mb == 0:
            g.wait_ge(sm.h["cGP"], k.vecs_done_gp)
        for hh in range(2):
            sm.dma(g.dma_start(
                out=k.wbuf[:, hh * 8 * 512:(hh + 1) * 8 * 512]
                    .rearrange("p (c w) -> p c w", c=8),
                in_=k.v_woT[:, hh * 8:(hh + 1) * 8,
                            mb * 512:(mb + 1) * 512]), "dWW")
        wo_loaded = sm.val("dWW")
        for j in range(4):
            r0 = sb * 512 + j * 128
            xs = k.xstr[pscnt % 2]
            if xs_guard[pscnt % 2]:
                sy.wait_ge(sm.h["cVE"], xs_guard[pscnt % 2])
            sm.dma(sy.dma_start(out=xs[:],
                                in_=k.d_x[r0:r0 + 128, mb * 512:(mb + 1) * 512]),
                   "dXT")
            xld = sm.val("dXT")
            ps = [k.psS0, k.psS1][pscnt % 2]
            te.wait_ge(sm.h["dWW"], wo_loaded)
            te.wait_ge(sm.h["cEV"], oT_done)
            if ps_guard[pscnt % 2]:
                te.wait_ge(sm.h["cVE"], ps_guard[pscnt % 2])
            elif k.tailps_guard[pscnt % 2]:
                te.wait_ge(sm.h["cVE"], k.tailps_guard[pscnt % 2])
            ins = None
            for oc in range(16):
                ins = te.matmul(
                    ps[:, 0:512],
                    k.oT_sb[:, oc * 512 + j * 128:oc * 512 + (j + 1) * 128],
                    k.wbuf[:, oc * 512:(oc + 1) * 512],
                    start=(oc == 0), stop=(oc == 15))
            tl = sm.inc(ins, "cTL")
            k.wbuf_guard = tl
            ve.wait_ge(sm.h["cTL"], tl)
            ve.wait_ge(sm.h["dXT"], xld)
            ve.tensor_mul(k.af32[:], ps[:, 0:512],
                          k.ga_full[:, mb * 512:(mb + 1) * 512])
            ps_guard[pscnt % 2] = sm.inc(
                ve.tensor_add(k.xnew[:, j * DM + mb * 512:
                                     j * DM + (mb + 1) * 512],
                              k.af32[:], xs[:]), "cVE")
            xs_guard[pscnt % 2] = ps_guard[pscnt % 2]
            pscnt += 1
    xnew_done = sm.val("cVE")
    k.tailps_guard = list(ps_guard)

    for j in range(4):
        xr = k.xnew[:, j * DM:(j + 1) * DM]
        ve.wait_ge(sm.h["cVE"], xnew_done)
        ve.reduce_sum(k.s2_sum[:], xr, axis=AX.X)
        se.wait_ge(sm.h["cVE"], xnew_done)
        if k.sqa_guard:
            se.wait_ge(sm.h["cVE"], k.sqa_guard)
        ins = None
        for i in range(4):
            ins = se.activation(out=k.psC[i][:, 0:512],
                                in_=xr[:, i * 512:(i + 1) * 512],
                                func=AF.Square, accum_out=k.s2_sqa[:, i:i + 1])
        sq = sm.inc(ins, "cSE")
        ve.tensor_scalar(k.s2_mu[:], k.s2_sum[:], 1.0 / DM, None, ALU.mult)
        ve.tensor_scalar(k.s2_mu2[:], k.s2_mu[:], k.s2_mu[:, 0:1], None,
                         ALU.mult)
        ve.wait_ge(sm.h["cSE"], sq)
        k.sqa_guard = sm.inc(ve.reduce_sum(k.s2_ssq[:], k.s2_sqa[:],
                                           axis=AX.X), "cVE")
        ve.tensor_scalar(k.s2_var[:], k.s2_ssq[:], 1.0 / DM, None, ALU.mult)
        vv = sm.inc(ve.tensor_sub(k.s2_var[:], k.s2_var[:], k.s2_mu2[:]),
                    "cVE")
        se.wait_ge(sm.h["cVE"], vv)
        sd = sm.inc(se.activation(out=k.s2_sd[:], in_=k.s2_var[:],
                                  func=AF.Sqrt, bias=k.eps5[:, 0:1]), "cSE")
        ve.wait_ge(sm.h["cSE"], sd)
        ve.reciprocal(k.s2_rstd[:], k.s2_sd[:])
        for st in range(4):
            sl = slice(st * 512, (st + 1) * 512)
            ve.tensor_scalar(k.h2[:, sl], xr[:, sl], k.s2_mu[:, 0:1],
                             k.s2_rstd[:, 0:1], ALU.subtract, ALU.mult)
            ve.tensor_mul(k.h2[:, sl], k.h2[:, sl], k.cb1_full[:, sl])
            ve.tensor_add(k.h2[:, sl], k.h2[:, sl], k.sb_full[:, sl])
            ve.reduce_max(k.s2_am4[:, st:st + 1], k.h2[:, sl], axis=AX.X,
                          apply_absolute_value=True)
        ve.reduce_max(k.s2_amax[:], k.s2_am4[:], axis=AX.X)
        ve.tensor_scalar(k.s2_amax[:], k.s2_amax[:], 1e-8, None, ALU.max)
        ve.reciprocal(k.s2_ra[:], k.s2_amax[:])
        ve.tensor_scalar(k.s2_r[:], k.s2_ra[:], 127.0, None, ALU.mult)
        ve.reciprocal(k.s2_ar[:], k.s2_r[:])
        ve.tensor_scalar(k.asw4[:, j:j + 1], k.s2_ar[:], k.ws_pb[:, 0:1],
                         None, ALU.mult)
        if k.xq_tr_last:
            ve.wait_ge(sm.h["cTE"], k.xq_tr_last)
        for st in range(4):
            sl = slice(st * 512, (st + 1) * 512)
            ve.tensor_scalar(k.qv[:], k.h2[:, sl], k.s2_r[:, 0:1], None,
                             ALU.mult)
            ve.tensor_copy(k.qi[:], k.qv[:])
            ve.tensor_copy(k.qf[:], k.qi[:])
            ve.tensor_sub(k.qe[:], k.qv[:], k.qf[:])
            ve.scalar_tensor_tensor(k.qt1[:], k.qe[:], 0.5, k.qf[:],
                                    ALU.is_gt, ALU.add)
            ve.scalar_tensor_tensor(k.qt1[:], k.qe[:], -0.5, k.qt1[:],
                                    ALU.is_ge, ALU.add)
            sm.inc(ve.tensor_scalar(k.xqtok[:, sl], k.qt1[:], -1.0, None,
                                    ALU.add), "cVE")
        xq_ready = sm.val("cVE")
        for hb in range(2):
            te.wait_ge(sm.h["cVE"], xq_ready)
            te.wait_ge(sm.h["cEV"], k.psTr_guard[hb])
            ins = None
            for i in range(8):
                ic = hb * 8 + i
                ins = te.transpose(psl(k, hb * 8 + i),
                                   k.xqtok[:, ic * 128:(ic + 1) * 128],
                                   k.identb[:])
            tr = sm.inc(ins, "cTE")
            k.xq_tr_last = tr
            ve.wait_ge(sm.h["cTE"], tr)
            k.psTr_guard[hb] = sm.inc(ve.tensor_copy(
                k.xqT[:, j * DM:(j + 1) * DM]
                    .rearrange("p (c w) -> p c w", c=16)
                    [:, hb * 8:(hb + 1) * 8, :],
                k.psTr[:, hb * 1024:hb * 1024 + 1024]
                    .rearrange("p (c w) -> p c w", c=8)), "cEV")
    xqT_done = sm.val("cEV")

    ps_guard = [0, 0]
    pscnt = 0
    for ob in range(4):
        g.wait_ge(sm.h["cTL"], k.wbuf_guard)
        for hh in range(2):
            sm.dma(g.dma_start(
                out=k.wbuf[:, hh * 8 * 512:(hh + 1) * 8 * 512]
                    .rearrange("p (c w) -> p c w", c=8),
                in_=k.v_wfq[:, hh * 8:(hh + 1) * 8,
                            ob * 512:(ob + 1) * 512]), "dWW")
        wf_loaded = sm.val("dWW")
        for j in range(4):
            ps = [k.psS0, k.psS1][pscnt % 2]
            te.wait_ge(sm.h["dWW"], wf_loaded)
            te.wait_ge(sm.h["cEV"], xqT_done)
            if ps_guard[pscnt % 2]:
                te.wait_ge(sm.h["cVE"], ps_guard[pscnt % 2])
            elif k.tailps_guard[pscnt % 2]:
                te.wait_ge(sm.h["cVE"], k.tailps_guard[pscnt % 2])
            ins = None
            for ic in range(16):
                ins = te.matmul(
                    ps[:, 0:512],
                    k.xqT[:, j * DM + ic * 128:j * DM + (ic + 1) * 128],
                    k.wbuf[:, ic * 512:(ic + 1) * 512],
                    start=(ic == 0), stop=(ic == 15))
            tl = sm.inc(ins, "cTL")
            k.wbuf_guard = tl
            ve.wait_ge(sm.h["cTL"], tl)
            if k.ostr_guard[pscnt % 2]:
                ve.wait_ge(sm.h["dO"], k.ostr_guard[pscnt % 2])
            sl = slice(ob * 512, (ob + 1) * 512)
            ve.tensor_scalar(k.af32[:], ps[:, 0:512], k.asw4[:, j:j + 1],
                             None, ALU.mult)
            ve.tensor_mul(k.af32[:], k.af32[:], k.gb_full[:, sl])
            ps_guard[pscnt % 2] = sm.inc(
                ve.tensor_add(k.ostr[pscnt % 2][:], k.af32[:],
                              k.xnew[:, j * DM + ob * 512:
                                     j * DM + (ob + 1) * 512]), "cVE")
            se.wait_ge(sm.h["cVE"], ps_guard[pscnt % 2])
            k.ostr_guard[pscnt % 2] = sm.dma(se.dma_start(
                out=k.d_y[(il * 4 + j) * 128:(il * 4 + j + 1) * 128, sl],
                in_=k.ostr[pscnt % 2][:]), "dO")
            pscnt += 1
    k.tailps_guard = list(ps_guard)


def emit_body2(k, half):
    sm = k.sm
    k.av_prev = 0
    k.ep_prev = 0
    k.ce_this = 0
    k.av_last = 0
    k.osb_done = 0
    k.ps_guard = [0, 0]
    k.ab_guard = [0, 0]
    k.kbuf_guard = [0, 0]
    k.psTr_guard = [0, 0]
    k.tailps_guard = [0, 0]
    k.ostr_guard = [0, 0]
    k.wbuf_guard = 0
    k.sqa_guard = 0
    k.xq_tr_last = 0
    for eng in [k.te, k.ve, k.se, k.sy, k.gp]:
        eng.wait_ge(sm.h["cGP"], k.negm_done)
        eng.wait_ge(sm.h["dSP"], k.all_spills)
    for il, sb in enumerate(OWN_SB[half]):
        if 2 * il >= P2LIM:
            break
        emit_attention_sb(k, il, sb)
        if 2 * il + 1 >= P2LIM:
            break
        emit_tail_sb(k, il, sb)
    k.sy.wait_ge(sm.h["dO"], sm.val("dO"))


def build_nc():
    nc = bass.Bass("TRN2", target_bir_lowering=False, debug=False)
    est = ExitStack()
    k = K()
    k.nc = nc
    sm = Sems()
    k.sm = sm

    hx = nc.dram_tensor("x", [S, DM], F32, kind="ExternalInput")
    hrole = nc.dram_tensor("role", [1, 1], I32, kind="ExternalInput")
    hcT = nc.dram_tensor("cT", [1, DM], F32, kind="ExternalInput")
    hwmod = nc.dram_tensor("wmod", [DM, 6 * DM], F32, kind="ExternalInput")
    hwqT = nc.dram_tensor("wqT", [DM, 2 * DM], F32, kind="ExternalInput")
    hwkT = nc.dram_tensor("wkT", [DM, 2 * DM], F32, kind="ExternalInput")
    hwvT = nc.dram_tensor("wvT", [DM, DM], F32, kind="ExternalInput")
    hwoT = nc.dram_tensor("woT", [DM, DM], F32, kind="ExternalInput")
    hwfT = nc.dram_tensor("wfT", [DM, DM], F32, kind="ExternalInput")
    hlva = nc.dram_tensor("lva", [1, 16], F32, kind="ExternalInput")
    hlvb = nc.dram_tensor("lvb", [1, 16], F32, kind="ExternalInput")
    htri = nc.dram_tensor("tri", [128, 128], BF16, kind="ExternalInput")
    hid = nc.dram_tensor("ident", [128, 128], BF16, kind="ExternalInput")
    hy = nc.dram_tensor("y", [S // 2, DM], F32, kind="ExternalOutput")
    if KDBG:
        k.d_dbg_vec = nc.dram_tensor("dbg_vec", [128, 128 + 4 * DM], F32,
                                     kind="ExternalOutput").ap()
        k.d_dbg_hT = nc.dram_tensor("dbg_hT", [DM, S], BF16,
                                    kind="ExternalOutput").ap()
        k.d_dbg_qT = nc.dram_tensor("dbg_qT", [2 * DM, S // 2], BF16,
                                    kind="ExternalOutput").ap()
        k.d_dbg_kT = nc.dram_tensor("dbg_kT", [2 * DM, S], BF16,
                                    kind="ExternalOutput").ap()
        k.d_dbg_vS = nc.dram_tensor("dbg_vS", [S, DM], BF16,
                                    kind="ExternalOutput").ap()
        k.d_dbg_wfq = nc.dram_tensor("dbg_wfq", [DM, DM], BF16,
                                     kind="ExternalOutput").ap()
    hhT = nc.dram_tensor("hT_s", [DM, S], BF16, kind="Internal")
    hqT = nc.dram_tensor("qT_s", [2 * DM, S // 2], BF16, kind="Internal")
    hkT = nc.dram_tensor("kT_s", [2 * DM, S], BF16, kind="Internal")
    hvS = nc.dram_tensor("v_s", [S, DM], BF16, kind="Internal")
    hwfq = nc.dram_tensor("wfq_s", [DM, DM], BF16, kind="Internal")

    k.d_x = hx.ap()
    k.d_role = hrole.ap()
    k.d_lva, k.d_lvb = hlva.ap(), hlvb.ap()
    k.d_tri, k.d_id = htri.ap(), hid.ap()
    k.d_y = hy.ap()
    k.d_cT = hcT.ap()
    k.v_x = k.d_x.rearrange("(t p) d -> p t d", p=128)
    k.v_wmod = hwmod.ap().rearrange("(c p) o -> p c o", p=128)
    k.v_wqT = hwqT.ap().rearrange("(c p) o -> p c o", p=128)
    k.v_wkT = hwkT.ap().rearrange("(c p) o -> p c o", p=128)
    k.v_wvT = hwvT.ap().rearrange("(c p) o -> p c o", p=128)
    k.v_woT = hwoT.ap().rearrange("(c p) o -> p c o", p=128)
    k.v_wfT = hwfT.ap().rearrange("(c p) o -> p c o", p=128)
    k.v_hTw = hhT.ap().rearrange("(c p) t -> p c t", p=128)
    k.v_qT = hqT.ap().rearrange("(c p) t -> p c t", p=128)
    k.v_kT = hkT.ap().rearrange("(c p) t -> p c t", p=128)
    k.v_vS = hvS.ap().rearrange("(t p) d -> p t d", p=128)
    k.v_vSr = hvS.ap().rearrange("(c p) d -> p c d", p=128)
    k.v_wfq = hwfq.ap().rearrange("(c p) o -> p c o", p=128)
    k.hT_all, k.qT_all, k.kT_all = hhT.ap(), hqT.ap(), hkT.ap()
    k.vS_all, k.wfq_all = hvS.ap(), hwfq.ap()

    for n in ["dC", "dM", "dX", "dW", "dF", "dSP", "dK", "dV", "dQ", "dXT",
              "dWW", "dO", "dH",
              "cSE", "cVE", "cTE", "cEV", "cPJ", "cNM", "cQN",
              "cSC", "cAX", "cMK", "cAV", "cCE", "cEP", "cTL",
              "cQ2", "cGP"]:
        sm.add(n, est.enter_context(nc.semaphore(n)))

    k.ve_raw = nc.vector
    k.te, k.ve, k.se, k.gp, k.sy = (nc.tensor, VEGuard(nc.vector), nc.scalar,
                                    nc.gpsimd, nc.sync)

    def sbuf(name, shape, dt):
        return est.enter_context(nc.sbuf_tensor(name, shape, dt))

    def psum(name, shape, dt):
        return est.enter_context(nc.psum_tensor(name, shape, dt))

    k.psS0 = psum("psS0", [128, 512], F32)
    k.psS1 = psum("psS1", [128, 512], F32)
    k.psC = [psum(f"psC{j}", [128, 512], F32) for j in range(4)]
    k.psTr = psum("psTr", [128, 2048], BF16)

    k.rs = sbuf("rs", [1, 1], I32)
    k.ct = sbuf("ct", [128, 16], F32)
    k.ctb = sbuf("ctb", [128, 16], BF16)
    k.lva = sbuf("lva_s", [1, 16], F32)
    k.lvb = sbuf("lvb_s", [1, 16], F32)
    k.trim = sbuf("trim", [128, 128], BF16)
    k.identb = sbuf("identb", [128, 128], BF16)
    k.ones_b = sbuf("ones_b", [128, 1], BF16)
    k.onesf = sbuf("onesf", [128, 1], F32)
    k.modT = sbuf("modT", [128, 32], F32)
    k.ca1v = sbuf("ca1v", [128, 16], F32)
    k.ga_full = sbuf("ga_full", [128, DM], F32)
    k.gb_full = sbuf("gb_full", [128, DM], F32)
    k.cb1_full = sbuf("cb1_full", [128, DM], F32)
    k.sb_full = sbuf("sb_full", [128, DM], F32)
    k.lam_pb = sbuf("lam_pb", [128, 1], F32)
    k.ws_pb = sbuf("ws_pb", [128, 1], F32)
    k.rws_pb = sbuf("rws_pb", [128, 1], F32)
    k.negm_pb = sbuf("negm_pb", [128, 16], F32)
    k.maxq2 = sbuf("maxq2", [1, 16], F32)
    k.maxk2 = sbuf("maxk2", [1, 16], F32)
    k.tiny = sbuf("tiny", [1, 16], F32)
    k.tiny2 = sbuf("tiny2", [1, 16], F32)
    k.wsum = sbuf("wsum", [128, 1], F32)
    k.eps5 = sbuf("eps5", [128, 1], F32)
    k.eps6 = sbuf("eps6", [128, 1], F32)
    k.onesr = sbuf("onesr", [1, 128], F32)
    if KDBG:
        k.dbgst = sbuf("dbgst", [128, 12], F32)
        k.dbgstage = sbuf("dbgstage", [128, 128], F32)
    k.bc_cnt = 0
    k.bc_hist = [0, 0]
    k.bc_te_last = 0
    k.wfred = sbuf("wfred", [128, 1], F32)

    with nc.Block() as block:
        @block.gpsimd
        def _(gph):
            emit_consts(k)
            if PHASE >= 1:
                with ExitStack() as p0:
                    emit_p0(k, p0)
            with ExitStack() as p1:
                def sb1(name, shape, dt):
                    return p1.enter_context(nc.sbuf_tensor(name, shape, dt))
                if PHASE >= 2:
                    emit_p1a(k, p1)
                k.wres = sb1("wres", [128, 16 * 2048], BF16)
                k.htsl = [sb1(f"htsl{i}", [128, 16 * 512], BF16)
                          for i in range(2)]
                k.stg = [sb1(f"stg{i}", [128, 512], BF16) for i in range(4)]
                k.sqt = [sb1(f"sqt{i}", [128, 512], BF16) for i in range(2)]
                k.vts = [sb1(f"vts{i}", [128, 512], BF16) for i in range(2)]
                k.wfbuf2 = sb1("wfbuf2", [128, 1024], F32)
                k.qv = sb1("qv", [128, 512], F32)
                k.qi = sb1("qi", [128, 512], I32)
                k.qf = sb1("qf", [128, 512], F32)
                k.qe = sb1("qe", [128, 512], F32)
                k.qt1 = sb1("qt1", [128, 512], F32)
                k.wqstage = sb1("wqstage", [128, DM], BF16)

                kblocks = [(nb, nb) for nb in range(8)]
                if PHASE >= 3:
                    emit_proj_pass(k, k.v_wkT, 0, 16, kblocks, k.v_kT, k.maxk2)
                    emit_proj_pass(k, k.v_wkT, 16, 32, kblocks, k.v_kT, k.maxk2)
                if PHASE >= 4:
                    emit_v_pass(k)
                if PHASE >= 5:
                    emit_wfq(k)
                if PHASE >= 6:
                    snap = sm.snap()
                    ends = []
                    for i in nc.Switch(engines=k.sw_engines, index=k.idx, n=2):
                        sm.restore(snap)
                        own = OWN_SB[i]
                        qblocks = [(own[t], t) for t in range(4)]
                        emit_proj_pass(k, k.v_wqT, 0, 16, qblocks, k.v_qT,
                                       k.maxq2)
                        emit_proj_pass(k, k.v_wqT, 16, 32, qblocks, k.v_qT,
                                       k.maxq2)
                        ends.append(sm.snap())
                    assert ends[0] == ends[1], "switch1 asymmetric"
                    emit_negm(k)
                k.all_spills = sm.val("dSP")
                if KDBG:
                    emit_debug_dumps(k)

            with ExitStack() as p2:
                def sb2(name, shape, dt):
                    return p2.enter_context(nc.sbuf_tensor(name, shape, dt))
                k.kbuf = [sb2(f"kbuf{i}", [128, 2 * 1024], BF16)
                          for i in range(2)]
                k.vsb = sb2("vsb", [128, 32 * 257], BF16)
                k.qslab = sb2("qslab", [128, 2048], BF16)
                k.abuf = [sb2(f"abuf{i}", [128, 512], BF16) for i in range(2)]
                k.Csb = sb2("Csb", [128, 4 * 257], F32)
                k.rc0 = sb2("rc0", [128, 1], F32)
                k.rc1 = sb2("rc1", [128, 1], F32)
                k.t1f = sb2("t1f", [128, 256], F32)
                k.off = sb2("off", [128, 256], F32)
                k.sqf = sb2("sqf", [128, 256], F32)
                k.grp8 = sb2("grp8", [128, 8], F32)
                k.grs = sb2("grs", [128, 8], F32)
                k.grr = sb2("grr", [128, 8], F32)
                k.o_sb = sb2("o_sb", [128, 4 * DM], BF16)
                k.oT_sb = sb2("oT_sb", [128, 16 * 512], BF16)
                k.xnew = sb2("xnew", [128, 4 * DM], F32)
                k.xstr = [sb2(f"xstr{i}", [128, 512], F32) for i in range(2)]
                k.wbuf = sb2("wbuf", [128, 16 * 512], BF16)
                k.af32 = sb2("af32", [128, 512], F32)
                k.xqT = sb2("xqT", [128, 4 * DM], BF16)
                k.xqtok = sb2("xqtok", [128, DM], BF16)
                k.h2 = sb2("h2", [128, DM], F32)
                k.s2_sum = sb2("s2_sum", [128, 1], F32)
                k.s2_mu = sb2("s2_mu", [128, 1], F32)
                k.s2_mu2 = sb2("s2_mu2", [128, 1], F32)
                k.s2_sqa = sb2("s2_sqa", [128, 4], F32)
                k.s2_ssq = sb2("s2_ssq", [128, 1], F32)
                k.s2_var = sb2("s2_var", [128, 1], F32)
                k.s2_sd = sb2("s2_sd", [128, 1], F32)
                k.s2_rstd = sb2("s2_rstd", [128, 1], F32)
                k.s2_am4 = sb2("s2_am4", [128, 4], F32)
                k.s2_amax = sb2("s2_amax", [128, 1], F32)
                k.s2_ra = sb2("s2_ra", [128, 1], F32)
                k.s2_r = sb2("s2_r", [128, 1], F32)
                k.s2_ar = sb2("s2_ar", [128, 1], F32)
                k.asw4 = sb2("asw4", [128, 4], F32)
                k.ostr = [sb2(f"ostr{i}", [128, 512], F32) for i in range(2)]
                k.qv = sb2("qv2", [128, 512], F32)
                k.qi = sb2("qi2", [128, 512], I32)
                k.qf = sb2("qf2", [128, 512], F32)
                k.qe = sb2("qe2", [128, 512], F32)
                k.qt1 = sb2("qt12", [128, 512], F32)

                k.ve.wait_ge(sm.h["dSP"], k.all_spills)
                ins = None
                for kc in range(32):
                    ins = k.ve.memset(k.vsb[:, kc * 257 + 256:kc * 257 + 257],
                                      1.0)
                k.vsb_ones = sm.inc(ins, "cVE")

                if PHASE >= 7:
                    snap = sm.snap()
                    for i in nc.Switch(engines=k.sw_engines, index=k.idx, n=2):
                        sm.restore(snap)
                        emit_body2(k, i)

    est.close()
    return nc


_NC_CACHE = {}


def _get_nc():
    if "nc" not in _NC_CACHE:
        _NC_CACHE["nc"] = build_nc()
    return _NC_CACHE["nc"]


def kernel(L0_attn_x, L0_attn_l_q1, L0_attn_l_k1, L0_attn_l_q2, L0_attn_l_k2,
           L0_attn_wq, L0_attn_wk, L0_attn_wv, L0_attn_wo, L0_ffn_weight,
           L0_c, L0_wmod):
    nc = _get_nc()
    f32 = np.float32
    bf16 = ml_dtypes.bfloat16
    wqT = np.ascontiguousarray(np.asarray(L0_attn_wq, f32).T)
    wkT = np.ascontiguousarray(np.asarray(L0_attn_wk, f32).T)
    wvT = np.ascontiguousarray(np.asarray(L0_attn_wv, f32).T)
    woT = np.ascontiguousarray(np.asarray(L0_attn_wo, f32).T)
    wfT = np.ascontiguousarray(np.asarray(L0_ffn_weight, f32).T)
    wmod = np.ascontiguousarray(np.asarray(L0_wmod, f32))
    lva = np.ascontiguousarray(np.concatenate(
        [np.asarray(L0_attn_l_q1, f32), np.asarray(L0_attn_l_q2, f32)])[None, :])
    lvb = np.ascontiguousarray(np.concatenate(
        [np.asarray(L0_attn_l_k1, f32), np.asarray(L0_attn_l_k2, f32)])[None, :])
    tri = np.triu(np.ones((128, 128), dtype=bf16))
    ident = np.eye(128, dtype=bf16)
    x = np.asarray(L0_attn_x, f32)
    cmat = np.asarray(L0_c, f32)

    in_maps = []
    for c in range(8):
        b, half = c // 2, c % 2
        in_maps.append({
            "x": np.ascontiguousarray(x[b]),
            "role": np.array([[half]], np.int32),
            "cT": np.ascontiguousarray(cmat[b][None, :]),
            "wmod": wmod,
            "wqT": wqT, "wkT": wkT, "wvT": wvT, "woT": woT, "wfT": wfT,
            "lva": lva, "lvb": lvb, "tri": tri, "ident": ident,
        })
    res = run_bass_kernel_spmd(nc, in_maps, core_ids=list(range(8)))
    y = np.empty((4, S, DM), np.float32)
    for c in range(8):
        b, half = c // 2, c % 2
        yc = np.asarray(res.results[c]["y"])
        for i, sb in enumerate(OWN_SB[half]):
            y[b, sb * 512:(sb + 1) * 512] = yc[i * 512:(i + 1) * 512]
    return y


# version 02
# speedup vs baseline: 1.0363x; 1.0363x over previous
import math

import numpy as np
import ml_dtypes
from contextlib import ExitStack

import concourse.bass as bass
import concourse.mybir as mybir
from concourse.bass_utils import run_bass_kernel_spmd

F32 = mybir.dt.float32
BF16 = mybir.dt.bfloat16
I32 = mybir.dt.int32
AF = mybir.ActivationFunctionType
ALU = mybir.AluOpType
AX = mybir.AxisListType

import os
PHASE = int(os.environ.get("KPHASE", "99"))
P2LIM = int(os.environ.get("KP2", "99"))
KDBG = int(os.environ.get("KDBG", "0"))
S = 4096
DM = 2048
L_INIT = 0.8 - 0.6 * math.exp(-0.3)
OWN_SB = {0: [0, 1, 6, 7], 1: [2, 3, 4, 5]}


class Sems:
    def __init__(self):
        self.h = {}
        self.c = {}

    def add(self, name, handle):
        self.h[name] = handle
        self.c[name] = 0

    def inc(self, instr, name, n=1):
        instr.then_inc(self.h[name], n)
        self.c[name] += n
        return self.c[name]

    def dma(self, instr, name):
        return self.inc(instr, name, 16)

    def val(self, name):
        return self.c[name]

    def wt(self, eng, name, val=None):
        eng.wait_ge(self.h[name], self.c[name] if val is None else val)

    def snap(self):
        return dict(self.c)

    def restore(self, s):
        self.c = dict(s)


class K:
    pass


class VEGuard:

    _OPS = {"tensor_copy", "tensor_scalar", "tensor_single_scalar",
            "tensor_tensor", "tensor_add", "tensor_sub", "tensor_mul",
            "tensor_max", "reduce_sum", "reduce_max", "tensor_reduce",
            "reciprocal", "scalar_tensor_tensor", "memset", "select",
            "tensor_scalar_add", "tensor_scalar_sub", "tensor_scalar_mul",
            "tensor_scalar_max", "tensor_scalar_min"}

    def __init__(self, ve):
        self._ve = ve

    def __getattr__(self, name):
        f = getattr(self._ve, name)
        if name not in self._OPS:
            return f

        def wrapped(out, *a, **kw):
            ins = f(out, *a, **kw)
            try:
                small = out.free_size() < 64
            except Exception:
                small = True
            if small:
                self._ve.drain()
            return ins
        return wrapped


def psl(k, slot):
    return k.psTr[:, slot * 128:(slot + 1) * 128]


def bcast(k, dst_ap, src_row_ap, n):
    sm, te, ve = k.sm, k.te, k.ve
    ps = k.psC[2 + (k.bc_cnt % 2)]
    if k.bc_hist[k.bc_cnt % 2]:
        te.wait_ge(sm.h["cGP"], k.bc_hist[k.bc_cnt % 2])
    sm.wt(te, "cVE")
    k.bc_te_last = sm.inc(te.matmul(ps[:, 0:n], k.onesr[0:1, :], src_row_ap,
                                    start=True, stop=True), "cTE")
    sm.wt(ve, "cTE")
    done = sm.inc(ve.tensor_copy(dst_ap, ps[:, 0:n]), "cGP")
    k.bc_hist[k.bc_cnt % 2] = done
    k.bc_cnt += 1
    return done


def emit_consts(k):
    sm, g, te, ve, se, sy = k.sm, k.gp, k.te, k.ve, k.se, k.sy
    for out, in_ in [
        (k.rs[:], k.d_role), (k.lva[:], k.d_lva),
        (k.lvb[:], k.d_lvb), (k.trim[:], k.d_tri), (k.identb[:], k.d_id),
    ]:
        sm.dma(g.dma_start(out=out, in_=in_), "dC")
    engines = [te, k.ve_raw, se, g, sy]
    k.idx = []
    for eng in engines:
        sm.wt(eng, "dC")
        r = eng.alloc_register()
        eng.reg_load(r, k.rs[:1, :1])
        k.idx.append(eng.compute_val(r))
    k.sw_engines = engines

    ve.memset(k.ones_b[:], 1.0)
    ve.memset(k.onesr[:], 1.0)
    ve.memset(k.eps5[:], 1e-5)
    ve.memset(k.eps6[:], 1e-6)
    k.onesf_done = sm.inc(ve.memset(k.onesf[:], 1.0), "cVE")
    ve.memset(k.maxq2[:], 0.0)
    ve.memset(k.maxk2[:], 0.0)
    ve.memset(k.wsum[:], 0.0)

    ve.tensor_mul(k.tiny[:1, 0:16], k.lva[:1, 0:16], k.lvb[:1, 0:16])
    sm.inc(ve.reduce_sum(k.tiny2[:1, 0:2],
                         k.tiny[:1, 0:16].rearrange("p (a b) -> p a b", a=2),
                         axis=AX.X), "cVE")
    sm.wt(se, "cVE")
    sm.inc(se.activation(out=k.tiny2[:1, 4:6], in_=k.tiny2[:1, 0:2],
                         func=AF.Exp), "cSE")
    sm.wt(ve, "cSE")
    ve.tensor_sub(k.tiny2[:1, 8:9], k.tiny2[:1, 4:5], k.tiny2[:1, 5:6])
    sm.inc(ve.tensor_scalar(k.tiny2[:1, 8:9], k.tiny2[:1, 8:9], L_INIT,
                            None, ALU.add), "cVE")
    bcast(k, k.lam_pb[:], k.tiny2[:1, 8:9], 1)



def emit_p0(k, p0):
    sm, te, ve, se, g = k.sm, k.te, k.ve, k.se, k.gp

    def sb0(name, shape, dt):
        return p0.enter_context(k.nc.sbuf_tensor(name, shape, dt))

    wmbuf = [sb0(f"wmbuf{i}", [128, 16 * 512], BF16) for i in range(2)]
    ct_row = sb0("ct_row", [1, DM], F32)
    modrow = [sb0(f"modrow{i}", [1, 512], F32) for i in range(2)]
    growS = sb0("growS", [1, 512], F32)
    grow2S = sb0("grow2S", [1, 512], F32)
    wfbuf = sb0("wfbuf0", [128, DM], F32)

    sm.dma(g.dma_start(out=ct_row[:], in_=k.d_cT), "dC")
    sm.wt(te, "dC")
    te.wait_ge(sm.h["cVE"], k.onesf_done)
    t_ins = None
    for i in range(16):
        t_ins = te.transpose(k.psS0[:, i:i + 1],
                             ct_row[:1, i * 128:(i + 1) * 128],
                             k.onesf[0:1, 0:1])
    sm.inc(t_ins, "cTE")
    sm.wt(ve, "cTE")
    k.ct_ev = sm.inc(ve.tensor_copy(k.ct[:], k.psS0[:, 0:16]), "cEV")
    sm.wt(se, "cEV")
    sm.inc(se.activation(out=k.ctb[:], in_=k.ct[:], func=AF.Silu), "cSE")
    k.silu_done = sm.val("cSE")

    slot_free = [0, 0]
    row_free = [0, 0]
    mod_ev = []

    def gate_transform(r, dst_full, off):
        if k.bc_te_last:
            ve.wait_ge(sm.h["cTE"], k.bc_te_last)
        ve.tensor_scalar(growS[:1, :], r, 0.0, None, ALU.is_ge)
        ve.tensor_scalar(growS[:1, :], growS[:1, :], 2.0, -1.0,
                         ALU.mult, ALU.add)
        ve.tensor_mul(grow2S[:1, :], r, growS[:1, :])
        ve.tensor_scalar(grow2S[:1, :], grow2S[:1, :], 0.25, None, ALU.max)
        sm.inc(ve.tensor_mul(growS[:1, :], growS[:1, :], grow2S[:1, :]), "cVE")
        bcast(k, dst_full[:, off * 512:(off + 1) * 512], growS[:1, :], 512)

    for oc in range(24):
        buf = wmbuf[oc % 2]
        if oc >= 2:
            g.wait_ge(sm.h["cTE"], slot_free[oc % 2])
        dm = f"dM{oc % 2}"
        for hh in range(2):
            sm.dma(g.dma_start(
                out=buf[:, hh * 8 * 512:(hh + 1) * 8 * 512]
                    .rearrange("p (c o) -> p c o", c=8),
                in_=k.v_wmod[:, hh * 8:(hh + 1) * 8, oc * 512:(oc + 1) * 512]),
                dm)
        sm.wt(te, dm)
        te.wait_ge(sm.h["cSE"], k.silu_done)
        if oc >= 2:
            te.wait_ge(sm.h["cEV"], mod_ev[oc - 2])
        else:
            te.wait_ge(sm.h["cEV"], k.ct_ev)
        ins = None
        for kc in range(16):
            ins = te.matmul(k.psS0[0:1, 0:512], k.ctb[:, kc:kc + 1],
                            buf[:, kc * 512:(kc + 1) * 512],
                            start=(kc == 0), stop=(kc == 15))
        slot_free[oc % 2] = sm.inc(ins, "cTE")
        sm.wt(ve, "cTE")
        if oc >= 2 and oc - 2 < 8:
            ve.wait_ge(sm.h["cTE"], row_free[oc % 2])
        if oc >= 2 and oc - 2 >= 8:
            ve.wait_ge(sm.h["cTE"], k.bc_te_last)
        mod_ev.append(sm.inc(
            ve.tensor_copy(modrow[oc % 2][:1, :], k.psS0[0:1, 0:512]), "cEV"))
        r = modrow[oc % 2][:1, :]
        if oc < 8:
            sm.wt(te, "cEV")
            t_ins = None
            for i in range(4):
                t_ins = te.transpose(k.psS1[:, oc * 4 + i:oc * 4 + i + 1],
                                     r[:, i * 128:(i + 1) * 128],
                                     k.onesf[0:1, 0:1])
            row_free[oc % 2] = sm.inc(t_ins, "cTE")
            if oc == 7:
                sm.wt(ve, "cTE")
                sm.inc(ve.tensor_copy(k.modT[:], k.psS1[:, 0:32]), "cEV")
                sm.inc(ve.tensor_scalar(k.ca1v[:], k.modT[:, 16:32], 1.0,
                                        None, ALU.add), "cVE")
                k.modT_done = sm.val("cVE")
        elif oc < 12:
            gate_transform(r, k.ga_full, oc - 8)
        elif oc < 16:
            bcast(k, k.sb_full[:, (oc - 12) * 512:(oc - 11) * 512],
                  r, 512)
        elif oc < 20:
            if k.bc_te_last:
                ve.wait_ge(sm.h["cTE"], k.bc_te_last)
            sm.inc(ve.tensor_scalar(growS[:1, :], r, 1.0, None, ALU.add),
                   "cVE")
            bcast(k, k.cb1_full[:, (oc - 16) * 512:(oc - 15) * 512],
                  growS[:1, :], 512)
        else:
            gate_transform(r, k.gb_full, oc - 20)
    k.vecs_done_gp = sm.val("cGP")

    wf_ve = 0
    for sl in range(16):
        if sl >= 1:
            g.wait_ge(sm.h["cVE"], wf_ve)
        sm.dma(g.dma_start(out=wfbuf[:], in_=k.v_wfT[:, sl:sl + 1, :]), "dF")
        sm.wt(ve, "dF")
        ve.reduce_sum(k.wfred[:], wfbuf[:], axis=AX.X,
                      apply_absolute_value=True)
        wf_ve = sm.inc(ve.tensor_add(k.wsum[:], k.wsum[:], k.wfred[:]), "cVE")
    sm.wt(te, "cVE")
    sm.wt(te, "cEV")
    sm.inc(te.matmul(k.psS1[0:1, 0:1], k.wsum[:], k.onesf[:],
                     start=True, stop=True), "cTE")
    sm.wt(ve, "cTE")
    ve.tensor_scalar(k.tiny[:1, 0:1], k.psS1[0:1, 0:1],
                     1.0 / float(DM * DM), 1e-6, ALU.mult, ALU.max)
    sm.inc(ve.reciprocal(k.tiny[:1, 1:2], k.tiny[:1, 0:1]), "cVE")
    bcast(k, k.ws_pb[:], k.tiny[:1, 0:1], 1)
    bcast(k, k.rws_pb[:], k.tiny[:1, 1:2], 1)
    k.ws_done = sm.val("cGP")


def emit_p1a(k, p1):
    sm, te, ve, se, sy = k.sm, k.te, k.ve, k.se, k.sy

    def sb1(name, shape, dt):
        return p1a_stack.enter_context(k.nc.sbuf_tensor(name, shape, dt))

    p1a_stack = ExitStack()
    xt = [sb1(f"xt{i}", [128, DM], F32) for i in range(2)]
    hts = [sb1(f"hts{i}", [128, 16 * 128], BF16) for i in range(2)]
    s_sum = sb1("s_sum", [128, 1], F32)
    s_mu = sb1("s_mu", [128, 1], F32)
    s_mu2 = sb1("s_mu2", [128, 1], F32)
    s_sqa = sb1("s_sqa", [128, 4], F32)
    s_ssq = sb1("s_ssq", [128, 1], F32)
    s_var = sb1("s_var", [128, 1], F32)
    s_sd = sb1("s_sd", [128, 1], F32)
    s_rstd = sb1("s_rstd", [128, 1], F32)
    xnb2 = [sb1(f"xnb{i}", [128, DM], BF16) for i in range(2)]

    norm_done = []
    hts_spill = []
    tr_evac = []
    xnb_tr = []

    sy.wait_ge(sm.h["cGP"], k.ws_done)
    sm.dma(sy.dma_start(out=xt[0][:], in_=k.v_x[:, 0:1, :]), "dX0")
    sm.dma(sy.dma_start(out=xt[1][:], in_=k.v_x[:, 1:2, :]), "dX1")
    for tt in range(32):
        b = xt[tt % 2]
        dx = f"dX{tt % 2}"
        sm.wt(ve, dx)
        ve.reduce_sum(s_sum[:], b[:], axis=AX.X)
        sm.wt(se, dx)
        if tt >= 1:
            se.wait_ge(sm.h["cVE"], norm_done[tt - 1])
        ins = None
        for i in range(4):
            ins = se.activation(out=k.psC[i][:, 0:512],
                                in_=b[:, i * 512:(i + 1) * 512],
                                func=AF.Square, accum_out=s_sqa[:, i:i + 1])
        sq_done = sm.inc(ins, "cSE")
        ve.tensor_scalar(s_mu[:], s_sum[:], 1.0 / DM, None, ALU.mult)
        ve.tensor_scalar(s_mu2[:], s_mu[:], s_mu[:, 0:1], None, ALU.mult)
        ve.wait_ge(sm.h["cSE"], sq_done)
        ve.reduce_sum(s_ssq[:], s_sqa[:], axis=AX.X)
        ve.tensor_scalar(s_var[:], s_ssq[:], 1.0 / DM, None, ALU.mult)
        sm.inc(ve.tensor_sub(s_var[:], s_var[:], s_mu2[:]), "cVE")
        sm.wt(se, "cVE")
        sd_done = sm.inc(se.activation(out=s_sd[:], in_=s_var[:],
                                       func=AF.Sqrt, bias=k.eps5[:, 0:1]), "cSE")
        ve.wait_ge(sm.h["cSE"], sd_done)
        ve.reciprocal(s_rstd[:], s_sd[:])
        if KDBG and tt == 0:
            ve.tensor_copy(k.dbgst[:, 0:1], s_sum[:])
            ve.tensor_copy(k.dbgst[:, 1:2], s_mu[:])
            ve.tensor_copy(k.dbgst[:, 2:3], s_mu2[:])
            ve.tensor_copy(k.dbgst[:, 3:4], s_ssq[:])
            ve.tensor_copy(k.dbgst[:, 4:5], s_var[:])
            ve.tensor_copy(k.dbgst[:, 5:6], s_sd[:])
            ve.tensor_copy(k.dbgst[:, 6:7], s_rstd[:])
            ve.tensor_copy(k.dbgst[:, 7:11], s_sqa[:])
        xnb = xnb2[tt % 2]
        if tt >= 2:
            ve.wait_ge(sm.h["cTE"], xnb_tr[tt - 2])
        nrm = sm.inc(ve.tensor_scalar(xnb[:], b[:], s_mu[:, 0:1],
                                      s_rstd[:, 0:1], ALU.subtract, ALU.mult),
                     "cVE")
        norm_done.append(nrm)
        if tt + 2 < 32:
            sy.wait_ge(sm.h["cVE"], nrm)
            sy.wait_ge(sm.h["cSE"], sq_done)
            sm.dma(sy.dma_start(out=xt[tt % 2][:],
                                in_=k.v_x[:, tt + 2:tt + 3, :]), f"dX{tt % 2}")
        hs = hts[tt % 2]
        for half in range(2):
            if len(tr_evac) >= 2:
                te.wait_ge(sm.h["cEV"], tr_evac[-2])
            te.wait_ge(sm.h["cVE"], nrm)
            ins = None
            for i in range(8):
                dc = half * 8 + i
                ins = te.transpose(psl(k, dc), xnb[:, dc * 128:(dc + 1) * 128],
                                   k.identb[:])
            tdone = sm.inc(ins, "cTE")
            if half == 1:
                xnb_tr.append(tdone)
            ve.wait_ge(sm.h["cTE"], tdone)
            if tt >= 2 and half == 0:
                ve.wait_ge(sm.h[f"dSH{tt % 2}"], hts_spill[tt - 2])
            if tt == 0 and half == 0:
                ve.wait_ge(sm.h["cVE"], k.modT_done)
            ev = None
            for i in range(8):
                dc = half * 8 + i
                ev = ve.tensor_scalar(hs[:, dc * 128:(dc + 1) * 128],
                                      psl(k, dc),
                                      k.ca1v[:, dc:dc + 1], k.modT[:, dc:dc + 1],
                                      ALU.mult, ALU.add)
            tr_evac.append(sm.inc(ev, "cEV"))
        se.wait_ge(sm.h["cEV"], tr_evac[-1])
        hts_spill.append(sm.dma(
            se.dma_start(out=k.v_hTw[:, :, tt * 128:(tt + 1) * 128],
                         in_=hs[:].rearrange("p (c w) -> p c w", c=16)),
            f"dSH{tt % 2}"))
    p1a_stack.close()


def emit_proj_pass(k, wview, m0, m1, blocks, spill_view, norm_dst):
    sm, te, ve, se, sy, g = k.sm, k.te, k.ve, k.se, k.sy, k.gp
    wres, htsl, stg, sqt = k.wres, k.htsl, k.stg, k.sqt

    nm = m1 - m0
    owidth = nm * 128
    g.wait_ge(sm.h["cPJ"], sm.val("cPJ"))
    sm.wt(g, "dSH0")
    sm.wt(g, "dSH1")
    for kc in range(16):
        sm.dma(g.dma_start(out=wres[:, kc * 2048:kc * 2048 + owidth],
                           in_=wview[:, kc:kc + 1, m0 * 128:m1 * 128]), "dW")
    w_loaded = sm.val("dW")
    psrot = [k.psS0, k.psS1]
    ev_of_ps = [0, 0]
    spill_of_stg = [0, 0, 0, 0]
    dh_block = []
    pj_block_end = []
    pscnt = 0
    nrm_hist = []
    nm_prev = sm.val("cNM")
    for ib, (nb, scol) in enumerate(blocks):
        slot = htsl[ib % 2]
        if ib >= 2:
            sy.wait_ge(sm.h["cPJ"], pj_block_end[ib - 2])
        if ib == 0:
            sm.wt(sy, "dSH0")
            sm.wt(sy, "dSH1")
            sy.wait_ge(sm.h["cPJ"], sm.val("cPJ"))
        elif ib == 1:
            sy.wait_ge(sm.h["cPJ"], sm.val("cPJ") - nm)
        dh = f"dH{ib % 2}"
        for hh in range(2):
            sm.dma(sy.dma_start(
                out=slot[:, hh * 8 * 512:(hh + 1) * 8 * 512]
                    .rearrange("p (c w) -> p c w", c=8),
                in_=k.v_hTw[:, hh * 8:(hh + 1) * 8,
                            nb * 512:(nb + 1) * 512]), dh)
        dh_block.append((dh, sm.val(dh)))
        for mp in range(nm // 2):
            for mo in (0, 1):
                m = m0 + 2 * mp + mo
                ps = psrot[pscnt % 2]
                if pscnt == 0:
                    te.wait_ge(sm.h["dW"], w_loaded)
                te.wait_ge(sm.h[dh_block[ib][0]], dh_block[ib][1])
                if ev_of_ps[pscnt % 2]:
                    te.wait_ge(sm.h["cEV"], ev_of_ps[pscnt % 2])
                ins = None
                lm = m - m0
                for kc in range(16):
                    ins = te.matmul(
                        ps[:, 0:512],
                        wres[:, kc * 2048 + lm * 128:kc * 2048 + (lm + 1) * 128],
                        slot[:, kc * 512:(kc + 1) * 512],
                        start=(kc == 0), stop=(kc == 15))
                pj = sm.inc(ins, "cPJ")
                sg = stg[pscnt % 4]
                ds = f"dS{pscnt % 4}"
                ve.wait_ge(sm.h["cPJ"], pj)
                if spill_of_stg[pscnt % 4]:
                    ve.wait_ge(sm.h[ds], spill_of_stg[pscnt % 4])
                ev_of_ps[pscnt % 2] = sm.inc(
                    ve.tensor_copy(sg[:], ps[:, 0:512]), "cEV")
                se.wait_ge(sm.h["cEV"], ev_of_ps[pscnt % 2])
                spill_of_stg[pscnt % 4] = sm.dma(
                    se.dma_start(out=spill_view[:, m:m + 1,
                                               scol * 512:(scol + 1) * 512],
                                 in_=sg[:]), ds)
                pscnt += 1
            if norm_dst is not None:
                hd = m0 // 2 + mp
                n = len(nrm_hist)
                psn = k.psC[n % 2]
                sg_e = stg[(pscnt - 2) % 4]
                sg_o = stg[(pscnt - 1) % 4]
                ve.wait_ge(sm.h["cEV"], ev_of_ps[(pscnt - 1) % 2])
                ve.wait_ge(sm.h["cNM"], nrm_hist[n - 2] if n >= 2 else nm_prev)
                ve.wait_ge(sm.h["cQ2"], sm.val("cQ2"))
                ve.tensor_mul(sqt[0][:], sg_e[:], sg_e[:])
                sq = sm.inc(ve.tensor_mul(sqt[1][:], sg_o[:], sg_o[:]), "cVE")
                te.wait_ge(sm.h["cVE"], sq)
                te.wait_ge(sm.h["cNM"], nrm_hist[n - 2] if n >= 2 else nm_prev)
                te.matmul(psn[0:1, 0:512], k.ones_b[:], sqt[0][:],
                          start=True, stop=False)
                q2 = sm.inc(te.matmul(psn[0:1, 0:512], k.ones_b[:], sqt[1][:],
                                      start=False, stop=True), "cQ2")
                ve.wait_ge(sm.h["cQ2"], q2)
                ve.reduce_max(k.tiny2[:1, 2:3], psn[0:1, 0:512], axis=AX.X)
                nrm_hist.append(sm.inc(
                    ve.tensor_max(norm_dst[:1, hd:hd + 1],
                                  norm_dst[:1, hd:hd + 1],
                                  k.tiny2[:1, 2:3]), "cNM"))
        pj_block_end.append(sm.val("cPJ"))


def emit_v_pass(k):
    sm, te, ve, se, sy, g = k.sm, k.te, k.ve, k.se, k.sy, k.gp
    wres, htsl, stg, vts = k.wres, k.htsl, k.stg, k.vts

    g.wait_ge(sm.h["cPJ"], sm.val("cPJ"))
    for kc in range(16):
        sm.dma(g.dma_start(out=wres[:, kc * 2048:kc * 2048 + DM],
                           in_=k.v_wvT[:, kc:kc + 1, :]), "dW")
    w_loaded = sm.val("dW")

    psrot = [k.psS0, k.psS1]
    ev_of_ps = [0, 0]
    trev_of_half = [0, 0]
    spill_of_vts = [0, 0]
    tr_hist = []
    dh_block = []
    pj_block_end = []
    pscnt = 0
    for nb in range(8):
        slot = htsl[nb % 2]
        if nb >= 2:
            sy.wait_ge(sm.h["cPJ"], pj_block_end[nb - 2])
        elif nb == 0:
            sy.wait_ge(sm.h["cPJ"], sm.val("cPJ"))
        elif nb == 1:
            sy.wait_ge(sm.h["cPJ"], sm.val("cPJ") - 16)
        dh = f"dH{nb % 2}"
        for hh in range(2):
            sm.dma(sy.dma_start(
                out=slot[:, hh * 8 * 512:(hh + 1) * 8 * 512]
                    .rearrange("p (c w) -> p c w", c=8),
                in_=k.v_hTw[:, hh * 8:(hh + 1) * 8,
                            nb * 512:(nb + 1) * 512]), dh)
        dh_block.append((dh, sm.val(dh)))
        for m in range(16):
            ps = psrot[pscnt % 2]
            if pscnt == 0:
                te.wait_ge(sm.h["dW"], w_loaded)
            te.wait_ge(sm.h[dh_block[nb][0]], dh_block[nb][1])
            if ev_of_ps[pscnt % 2]:
                te.wait_ge(sm.h["cEV"], ev_of_ps[pscnt % 2])
            ins = None
            for kc in range(16):
                ins = te.matmul(
                    ps[:, 0:512],
                    wres[:, kc * 2048 + m * 128:kc * 2048 + (m + 1) * 128],
                    slot[:, kc * 512:(kc + 1) * 512],
                    start=(kc == 0), stop=(kc == 15))
            pj = sm.inc(ins, "cPJ")
            sg = stg[pscnt % 4]
            ve.wait_ge(sm.h["cPJ"], pj)
            if len(tr_hist) >= 4:
                ve.wait_ge(sm.h["cTE"], tr_hist[-4])
            ev_of_ps[pscnt % 2] = sm.inc(ve.tensor_copy(sg[:], ps[:, 0:512]),
                                         "cEV")
            half = pscnt % 2
            te.wait_ge(sm.h["cEV"], ev_of_ps[pscnt % 2])
            if trev_of_half[half]:
                te.wait_ge(sm.h["cEV"], trev_of_half[half])
            ins = None
            for i in range(4):
                ins = te.transpose(psl(k, half * 8 + i),
                                   sg[:, i * 128:(i + 1) * 128], k.identb[:])
            tr = sm.inc(ins, "cTE")
            tr_hist.append(tr)
            vb = vts[pscnt % 2]
            ve.wait_ge(sm.h["cTE"], tr)
            if spill_of_vts[pscnt % 2]:
                ve.wait_ge(sm.h[f"dSV{pscnt % 2}"], spill_of_vts[pscnt % 2])
            ev = None
            for i in range(4):
                ev = ve.tensor_copy(vb[:, i * 128:(i + 1) * 128],
                                    psl(k, half * 8 + i))
            trev_of_half[half] = sm.inc(ev, "cEV")
            se.wait_ge(sm.h["cEV"], trev_of_half[half])
            spill_of_vts[pscnt % 2] = sm.dma(
                se.dma_start(out=k.v_vS[:, nb * 4:(nb + 1) * 4,
                                        m * 128:(m + 1) * 128],
                             in_=vb[:].rearrange("p (t w) -> p t w", t=4)),
                f"dSV{pscnt % 2}")
            pscnt += 1
        pj_block_end.append(sm.val("cPJ"))


def emit_wfq(k):
    sm, ve, se, g = k.sm, k.ve, k.se, k.gp
    wfbuf2, qv, qi, qf, qe, qt1, wqstage = (
        k.wfbuf2, k.qv, k.qi, k.qf, k.qe, k.qt1, k.wqstage)

    spill_prev = 0
    for sl2 in range(32):
        sl, hh = sl2 // 2, sl2 % 2
        if sl2 >= 1:
            g.wait_ge(sm.h["cQN"], sm.val("cQN"))
        if sl2 == 0:
            g.wait_ge(sm.h["cGP"], k.ws_done)
        sm.dma(g.dma_start(out=wfbuf2[:],
                           in_=k.v_wfT[:, sl:sl + 1, hh * 1024:(hh + 1) * 1024]),
               "dF")
        for st in range(2):
            src = wfbuf2[:, st * 512:(st + 1) * 512]
            sm.wt(ve, "dF")
            if spill_prev and st == 0 and hh == 0:
                ve.wait_ge(sm.h["dSQ"], spill_prev)
            ve.tensor_scalar(qv[:], src, k.rws_pb[:, 0:1], None, ALU.mult)
            ve.tensor_copy(qi[:], qv[:])
            ve.tensor_copy(qf[:], qi[:])
            ve.tensor_sub(qe[:], qv[:], qf[:])
            ve.scalar_tensor_tensor(qt1[:], qe[:], 0.5, qf[:],
                                    ALU.is_gt, ALU.add)
            ve.scalar_tensor_tensor(qt1[:], qe[:], -0.5, qt1[:],
                                    ALU.is_ge, ALU.add)
            ve.tensor_scalar(qt1[:], qt1[:], -1.0, 1.0, ALU.add, ALU.min)
            sm.inc(ve.tensor_scalar(
                wqstage[:, (hh * 2 + st) * 512:(hh * 2 + st + 1) * 512],
                qt1[:], -1.0, None, ALU.max), "cQN")
        if hh == 1:
            sm.wt(se, "cQN")
            spill_prev = sm.dma(se.dma_start(out=k.v_wfq[:, sl:sl + 1, :],
                                             in_=wqstage[:]), "dSQ")


def emit_negm(k):
    sm, ve, se, g = k.sm, k.ve, k.se, k.gp
    sm.wt(se, "cNM")
    se.activation(out=k.tiny[:1, 0:16], in_=k.maxq2[:1, 0:16], func=AF.Sqrt)
    sq = sm.inc(se.activation(out=k.tiny2[:1, 0:16], in_=k.maxk2[:1, 0:16],
                              func=AF.Sqrt), "cSE")
    ve.wait_ge(sm.h["cSE"], sq)
    ve.tensor_mul(k.tiny[:1, 0:16], k.tiny[:1, 0:16], k.tiny2[:1, 0:16])
    sm.inc(ve.tensor_scalar(k.tiny[:1, 0:16], k.tiny[:1, 0:16],
                            -1.001 / 16.0, None, ALU.mult), "cVE")
    bcast(k, k.negm_pb[:], k.tiny[:1, 0:16], 16)
    k.negm_done = sm.val("cGP")


def emit_debug_dumps(k):
    sm, g, ve = k.sm, k.gp, k.ve
    for eng in [g, ve]:
        for n, v in k.spill_sems:
            eng.wait_ge(sm.h[n], v)
        eng.wait_ge(sm.h["cGP"], k.negm_done)
    dv = k.d_dbg_vec
    stg = k.dbgstage
    for dst0, src in [
        (0, k.modT[:]), (32, k.ca1v[:]), (48, k.negm_pb[:]),
        (64, k.lam_pb[:]), (65, k.ws_pb[:]), (66, k.rws_pb[:]),
        (100, k.dbgst[:]),
    ]:
        w = src.shape[-1]
        ve.tensor_copy(stg[:, dst0:dst0 + w], src)
    ve.tensor_copy(stg[0:1, 67:83], k.maxq2[:1, :])
    sm.inc(ve.tensor_copy(stg[0:1, 83:99], k.maxk2[:1, :]), "cVE")
    sm.wt(g, "cVE")
    sm.dma(g.dma_start(out=dv[:, 0:128], in_=stg[:]), "dO")
    for out_sl, src in [
        (dv[:, 128:128 + DM], k.ga_full[:]),
        (dv[:, 128 + DM:128 + 2 * DM], k.gb_full[:]),
        (dv[:, 128 + 2 * DM:128 + 3 * DM], k.cb1_full[:]),
        (dv[:, 128 + 3 * DM:128 + 4 * DM], k.sb_full[:]),
    ]:
        sm.dma(g.dma_start(out=out_sl, in_=src), "dO")
    sm.dma(g.dma_start(out=k.d_dbg_hT, in_=k.hT_all), "dO")
    sm.dma(g.dma_start(out=k.d_dbg_qT, in_=k.qT_all), "dO")
    sm.dma(g.dma_start(out=k.d_dbg_kT, in_=k.kT_all), "dO")
    sm.dma(g.dma_start(out=k.d_dbg_vS, in_=k.vS_all), "dO")
    sm.dma(g.dma_start(out=k.d_dbg_wfq, in_=k.wfq_all), "dO")
    sm.wt(g, "dO")


def emit_attention_sb(k, il, sb):
    sm, te, ve, se, sy = k.sm, k.te, k.ve, k.se, k.sy
    KC = 4 * (sb + 1)
    NG = (KC + 7) // 8

    for hp in range(8):
        if k.av_prev:
            sy.wait_ge(sm.h["cAV"], k.av_prev)
        for c0 in range(0, KC, 8):
            c1 = min(c0 + 8, KC)
            sm.dma(sy.dma_start(
                out=k.vsb[:].rearrange("p (c w) -> p c w", w=257)
                    [:, c0:c1, 0:256],
                in_=k.v_vSr[:, c0:c1, hp * 256:(hp + 1) * 256]), "dV")
        vv_loaded = sm.val("dV")
        for g2 in range(2):
            sh = hp + 8 * g2
            sm.dma(sy.dma_start(
                out=k.qslab[:, g2 * 1024:(g2 + 1) * 1024]
                    .rearrange("p (c w) -> p c w", c=2),
                in_=k.v_qT[:, 2 * sh:2 * sh + 2, il * 512:(il + 1) * 512]),
                "dQ")
        qv_loaded = sm.val("dQ")

        for g2 in range(2):
            sh = hp + 8 * g2
            first_head = (g2 == 0)
            kg_loaded = []
            sc_group_end = [0] * NG
            exp_done = [0] * KC
            msk_done = [0] * KC
            av_of_kc = [0] * KC

            def load_group(kg):
                L = min(1024, KC * 128 - kg * 1024)
                if kg >= 2:
                    sy.wait_ge(sm.h["cSC"], sc_group_end[kg - 2])
                elif k.kbuf_guard[kg % 2]:
                    sy.wait_ge(sm.h["cSC"], k.kbuf_guard[kg % 2])
                dk = f"dK{kg % 2}"
                sm.dma(sy.dma_start(
                    out=k.kbuf[kg % 2][:]
                        .rearrange("p (c w) -> p c w", c=2)[:, :, 0:L],
                    in_=k.v_kT[:, 2 * sh:2 * sh + 2,
                               kg * 1024:kg * 1024 + L]), dk)
                kg_loaded.append((dk, sm.val(dk)))

            load_group(0)
            if NG > 1:
                load_group(1)

            def emit_S(kc):
                kg = kc // 8
                j0 = max(0, kc - 4 * sb)
                qoff = j0 * 128
                N = 512 - qoff
                ps = [k.psS0, k.psS1][kc % 2]
                te.wait_ge(sm.h[kg_loaded[kg][0]], kg_loaded[kg][1])
                if kc == 0:
                    te.wait_ge(sm.h["dQ"], qv_loaded)
                if kc >= 2:
                    te.wait_ge(sm.h["cAX"], exp_done[kc - 2])
                else:
                    if k.ps_guard[kc % 2]:
                        te.wait_ge(sm.h["cAX"], k.ps_guard[kc % 2])
                    if k.tailps_guard[kc % 2]:
                        te.wait_ge(sm.h["cVE"], k.tailps_guard[kc % 2])
                ins = None
                for ch in range(2):
                    ins = te.matmul(
                        ps[:, qoff:qoff + N],
                        k.kbuf[kg % 2][:, ch * 1024 + (kc % 8) * 128:
                                       ch * 1024 + (kc % 8 + 1) * 128],
                        k.qslab[:, g2 * 1024 + ch * 512 + qoff:
                                g2 * 1024 + ch * 512 + qoff + N],
                        start=(ch == 0), stop=(ch == 1))
                sc = sm.inc(ins, "cSC")
                sc_group_end[kg] = sc
                se.wait_ge(sm.h["cSC"], sc)
                ab_guard = (av_of_kc[kc - 2] if kc >= 2
                            else k.ab_guard[kc % 2])
                if ab_guard:
                    se.wait_ge(sm.h["cAV"], ab_guard)
                exp_done[kc] = sm.inc(se.activation(
                    out=k.abuf[kc % 2][:, qoff:qoff + N],
                    in_=ps[:, qoff:qoff + N], func=AF.Exp, scale=1.0 / 16.0,
                    bias=k.negm_pb[:, sh:sh + 1]), "cAX")
                if kc >= 4 * sb:
                    j = kc - 4 * sb
                    ve.wait_ge(sm.h["cAX"], exp_done[kc])
                    msk_done[kc] = sm.inc(ve.tensor_mul(
                        k.abuf[kc % 2][:, j * 128:(j + 1) * 128],
                        k.abuf[kc % 2][:, j * 128:(j + 1) * 128],
                        k.trim[:]), "cMK")
                if kc % 8 == 7 and (kg + 2) < NG:
                    load_group(kg + 2)

            def emit_attnV(kc):
                j0 = max(0, kc - 4 * sb)
                if kc >= 4 * sb:
                    te.wait_ge(sm.h["cMK"], msk_done[kc])
                else:
                    te.wait_ge(sm.h["cAX"], exp_done[kc])
                if kc == 0:
                    te.wait_ge(sm.h["dV"], vv_loaded)
                    if first_head:
                        if k.ep_prev:
                            te.wait_ge(sm.h["cEP"], k.ep_prev)
                        te.wait_ge(sm.h["cVE"], k.vsb_ones)
                    else:
                        te.wait_ge(sm.h["cCE"], k.ce_this)
                ins = None
                for j in range(j0, 4):
                    ins = te.matmul(
                        k.psC[j][:, 0:257],
                        k.abuf[kc % 2][:, j * 128:(j + 1) * 128],
                        k.vsb[:, kc * 257:kc * 257 + 257],
                        start=(kc == 0), stop=(kc == 4 * sb + j))
                av_of_kc[kc] = sm.inc(ins, "cAV")

            for kc in range(KC):
                emit_S(kc)
                if kc >= 1:
                    emit_attnV(kc - 1)
            emit_attnV(KC - 1)

            k.ps_guard[(KC - 1) % 2] = exp_done[KC - 1]
            if KC >= 2:
                k.ps_guard[KC % 2] = exp_done[KC - 2]
            k.ab_guard[(KC - 1) % 2] = av_of_kc[KC - 1]
            if KC >= 2:
                k.ab_guard[KC % 2] = av_of_kc[KC - 2]
            k.kbuf_guard[(NG - 1) % 2] = sc_group_end[NG - 1]
            if NG >= 2:
                k.kbuf_guard[NG % 2] = sc_group_end[NG - 2]
            k.av_last = av_of_kc[KC - 1]

            if first_head:
                ve.wait_ge(sm.h["cAV"], k.av_last)
                ev = None
                for j in range(4):
                    ev = ve.tensor_copy(k.Csb[:, j * 257:(j + 1) * 257],
                                        k.psC[j][:, 0:257])
                k.ce_this = sm.inc(ev, "cCE")

        ve.wait_ge(sm.h["cAV"], k.av_last)
        for j in range(4):
            c0 = k.Csb[:, j * 257:(j + 1) * 257]
            ve.reciprocal(k.rc0[:], c0[:, 256:257])
            ve.reciprocal(k.rc1[:], k.psC[j][:, 256:257])
            sm.inc(ve.tensor_scalar(k.t1f[:], k.psC[j][:, 0:256],
                                    k.rc1[:, 0:1], k.lam_pb[:, 0:1],
                                    ALU.mult, ALU.mult), "cEP")
            ve.scalar_tensor_tensor(k.off[:], c0[:, 0:256], k.rc0[:, 0:1],
                                    k.t1f[:], ALU.mult, ALU.subtract)
            ve.tensor_mul(k.sqf[:], k.off[:], k.off[:])
            grp = sm.inc(ve.reduce_sum(
                k.grp8[:], k.sqf[:].rearrange("p (a b) -> p a b", a=8),
                axis=AX.X), "cVE")
            se.wait_ge(sm.h["cVE"], grp)
            sdq = sm.inc(se.activation(out=k.grs[:], in_=k.grp8[:],
                                       func=AF.Sqrt, scale=1.0 / 32.0,
                                       bias=k.eps6[:, 0:1]), "cSE")
            ve.wait_ge(sm.h["cSE"], sdq)
            ve.reciprocal(k.grr[:], k.grs[:])
            sm.inc(ve.scalar_tensor_tensor(
                k.o_sb[:, j * DM + hp * 256:j * DM + (hp + 1) * 256]
                    .rearrange("p (a b) -> p a b", a=8),
                k.off[:].rearrange("p (a b) -> p a b", a=8),
                (1.0 - L_INIT),
                k.grr[:].rearrange("p (a b) -> p a b", b=1)
                    .broadcast_to([128, 8, 32]),
                ALU.mult, ALU.mult), "cVE")
        k.ep_prev = sm.val("cEP")
        k.av_prev = k.av_last
    k.osb_done = sm.val("cVE")


def emit_tail_sb(k, il, sb):
    sm, te, ve, se, sy, g = k.sm, k.te, k.ve, k.se, k.sy, k.gp

    tr_prev = [0, 0]
    for j in range(4):
        for hb in range(2):
            te.wait_ge(sm.h["cVE"], k.osb_done)
            if tr_prev[hb]:
                te.wait_ge(sm.h["cEV"], tr_prev[hb])
            elif k.psTr_guard[hb]:
                te.wait_ge(sm.h["cEV"], k.psTr_guard[hb])
            ins = None
            for i in range(8):
                oc = hb * 8 + i
                ins = te.transpose(psl(k, hb * 8 + i),
                                   k.o_sb[:, j * DM + oc * 128:
                                          j * DM + (oc + 1) * 128],
                                   k.identb[:])
            tr = sm.inc(ins, "cTE")
            ve.wait_ge(sm.h["cTE"], tr)
            tr_prev[hb] = sm.inc(ve.tensor_copy(
                k.oT_sb[:].rearrange("p (c w) -> p c w", w=512)
                    [:, hb * 8:(hb + 1) * 8, j * 128:(j + 1) * 128],
                k.psTr[:, hb * 1024:hb * 1024 + 1024]
                    .rearrange("p (c w) -> p c w", c=8)), "cEV")
    oT_done = sm.val("cEV")
    k.psTr_guard = list(tr_prev)

    ps_guard = [0, 0]
    xs_guard = [0, 0]
    pscnt = 0
    for mb in range(4):
        if k.wbuf_guard:
            g.wait_ge(sm.h["cTL"], k.wbuf_guard)
        if mb == 0:
            g.wait_ge(sm.h["cGP"], k.vecs_done_gp)
        for hh in range(2):
            sm.dma(g.dma_start(
                out=k.wbuf[:, hh * 8 * 512:(hh + 1) * 8 * 512]
                    .rearrange("p (c w) -> p c w", c=8),
                in_=k.v_woT[:, hh * 8:(hh + 1) * 8,
                            mb * 512:(mb + 1) * 512]), "dWW")
        wo_loaded = sm.val("dWW")
        for j in range(4):
            r0 = sb * 512 + j * 128
            xs = k.xstr[pscnt % 2]
            dxt = f"dXT{pscnt % 2}"
            if xs_guard[pscnt % 2]:
                sy.wait_ge(sm.h["cVE"], xs_guard[pscnt % 2])
            sm.dma(sy.dma_start(out=xs[:],
                                in_=k.d_x[r0:r0 + 128, mb * 512:(mb + 1) * 512]),
                   dxt)
            xld = (dxt, sm.val(dxt))
            ps = [k.psS0, k.psS1][pscnt % 2]
            te.wait_ge(sm.h["dWW"], wo_loaded)
            te.wait_ge(sm.h["cEV"], oT_done)
            if ps_guard[pscnt % 2]:
                te.wait_ge(sm.h["cVE"], ps_guard[pscnt % 2])
            elif k.tailps_guard[pscnt % 2]:
                te.wait_ge(sm.h["cVE"], k.tailps_guard[pscnt % 2])
            ins = None
            for oc in range(16):
                ins = te.matmul(
                    ps[:, 0:512],
                    k.oT_sb[:, oc * 512 + j * 128:oc * 512 + (j + 1) * 128],
                    k.wbuf[:, oc * 512:(oc + 1) * 512],
                    start=(oc == 0), stop=(oc == 15))
            tl = sm.inc(ins, "cTL")
            k.wbuf_guard = tl
            ve.wait_ge(sm.h["cTL"], tl)
            ve.wait_ge(sm.h[xld[0]], xld[1])
            ve.tensor_mul(k.af32[:], ps[:, 0:512],
                          k.ga_full[:, mb * 512:(mb + 1) * 512])
            ps_guard[pscnt % 2] = sm.inc(
                ve.tensor_add(k.xnew[:, j * DM + mb * 512:
                                     j * DM + (mb + 1) * 512],
                              k.af32[:], xs[:]), "cVE")
            xs_guard[pscnt % 2] = ps_guard[pscnt % 2]
            pscnt += 1
    xnew_done = sm.val("cVE")
    k.tailps_guard = list(ps_guard)

    for j in range(4):
        xr = k.xnew[:, j * DM:(j + 1) * DM]
        ve.wait_ge(sm.h["cVE"], xnew_done)
        ve.reduce_sum(k.s2_sum[:], xr, axis=AX.X)
        se.wait_ge(sm.h["cVE"], xnew_done)
        if k.sqa_guard:
            se.wait_ge(sm.h["cVE"], k.sqa_guard)
        ins = None
        for i in range(4):
            ins = se.activation(out=k.psC[i][:, 0:512],
                                in_=xr[:, i * 512:(i + 1) * 512],
                                func=AF.Square, accum_out=k.s2_sqa[:, i:i + 1])
        sq = sm.inc(ins, "cSE")
        ve.tensor_scalar(k.s2_mu[:], k.s2_sum[:], 1.0 / DM, None, ALU.mult)
        ve.tensor_scalar(k.s2_mu2[:], k.s2_mu[:], k.s2_mu[:, 0:1], None,
                         ALU.mult)
        ve.wait_ge(sm.h["cSE"], sq)
        k.sqa_guard = sm.inc(ve.reduce_sum(k.s2_ssq[:], k.s2_sqa[:],
                                           axis=AX.X), "cVE")
        ve.tensor_scalar(k.s2_var[:], k.s2_ssq[:], 1.0 / DM, None, ALU.mult)
        vv = sm.inc(ve.tensor_sub(k.s2_var[:], k.s2_var[:], k.s2_mu2[:]),
                    "cVE")
        se.wait_ge(sm.h["cVE"], vv)
        sd = sm.inc(se.activation(out=k.s2_sd[:], in_=k.s2_var[:],
                                  func=AF.Sqrt, bias=k.eps5[:, 0:1]), "cSE")
        ve.wait_ge(sm.h["cSE"], sd)
        ve.reciprocal(k.s2_rstd[:], k.s2_sd[:])
        for st in range(4):
            sl = slice(st * 512, (st + 1) * 512)
            ve.tensor_scalar(k.h2[:, sl], xr[:, sl], k.s2_mu[:, 0:1],
                             k.s2_rstd[:, 0:1], ALU.subtract, ALU.mult)
            ve.tensor_mul(k.h2[:, sl], k.h2[:, sl], k.cb1_full[:, sl])
            ve.tensor_add(k.h2[:, sl], k.h2[:, sl], k.sb_full[:, sl])
            ve.reduce_max(k.s2_am4[:, st:st + 1], k.h2[:, sl], axis=AX.X,
                          apply_absolute_value=True)
        ve.reduce_max(k.s2_amax[:], k.s2_am4[:], axis=AX.X)
        ve.tensor_scalar(k.s2_amax[:], k.s2_amax[:], 1e-8, None, ALU.max)
        ve.reciprocal(k.s2_ra[:], k.s2_amax[:])
        ve.tensor_scalar(k.s2_r[:], k.s2_ra[:], 127.0, None, ALU.mult)
        ve.reciprocal(k.s2_ar[:], k.s2_r[:])
        ve.tensor_scalar(k.asw4[:, j:j + 1], k.s2_ar[:], k.ws_pb[:, 0:1],
                         None, ALU.mult)
        if k.xq_tr_last:
            ve.wait_ge(sm.h["cTE"], k.xq_tr_last)
        for st in range(4):
            sl = slice(st * 512, (st + 1) * 512)
            ve.tensor_scalar(k.qv[:], k.h2[:, sl], k.s2_r[:, 0:1], None,
                             ALU.mult)
            ve.tensor_copy(k.qi[:], k.qv[:])
            ve.tensor_copy(k.qf[:], k.qi[:])
            ve.tensor_sub(k.qe[:], k.qv[:], k.qf[:])
            ve.scalar_tensor_tensor(k.qt1[:], k.qe[:], 0.5, k.qf[:],
                                    ALU.is_gt, ALU.add)
            ve.scalar_tensor_tensor(k.qt1[:], k.qe[:], -0.5, k.qt1[:],
                                    ALU.is_ge, ALU.add)
            sm.inc(ve.tensor_scalar(k.xqtok[:, sl], k.qt1[:], -1.0, None,
                                    ALU.add), "cVE")
        xq_ready = sm.val("cVE")
        for hb in range(2):
            te.wait_ge(sm.h["cVE"], xq_ready)
            te.wait_ge(sm.h["cEV"], k.psTr_guard[hb])
            ins = None
            for i in range(8):
                ic = hb * 8 + i
                ins = te.transpose(psl(k, hb * 8 + i),
                                   k.xqtok[:, ic * 128:(ic + 1) * 128],
                                   k.identb[:])
            tr = sm.inc(ins, "cTE")
            k.xq_tr_last = tr
            ve.wait_ge(sm.h["cTE"], tr)
            k.psTr_guard[hb] = sm.inc(ve.tensor_copy(
                k.xqT[:, j * DM:(j + 1) * DM]
                    .rearrange("p (c w) -> p c w", c=16)
                    [:, hb * 8:(hb + 1) * 8, :],
                k.psTr[:, hb * 1024:hb * 1024 + 1024]
                    .rearrange("p (c w) -> p c w", c=8)), "cEV")
    xqT_done = sm.val("cEV")

    ps_guard = [0, 0]
    pscnt = 0
    for ob in range(4):
        g.wait_ge(sm.h["cTL"], k.wbuf_guard)
        for hh in range(2):
            sm.dma(g.dma_start(
                out=k.wbuf[:, hh * 8 * 512:(hh + 1) * 8 * 512]
                    .rearrange("p (c w) -> p c w", c=8),
                in_=k.v_wfq[:, hh * 8:(hh + 1) * 8,
                            ob * 512:(ob + 1) * 512]), "dWW")
        wf_loaded = sm.val("dWW")
        for j in range(4):
            ps = [k.psS0, k.psS1][pscnt % 2]
            te.wait_ge(sm.h["dWW"], wf_loaded)
            te.wait_ge(sm.h["cEV"], xqT_done)
            if ps_guard[pscnt % 2]:
                te.wait_ge(sm.h["cVE"], ps_guard[pscnt % 2])
            elif k.tailps_guard[pscnt % 2]:
                te.wait_ge(sm.h["cVE"], k.tailps_guard[pscnt % 2])
            ins = None
            for ic in range(16):
                ins = te.matmul(
                    ps[:, 0:512],
                    k.xqT[:, j * DM + ic * 128:j * DM + (ic + 1) * 128],
                    k.wbuf[:, ic * 512:(ic + 1) * 512],
                    start=(ic == 0), stop=(ic == 15))
            tl = sm.inc(ins, "cTL")
            k.wbuf_guard = tl
            ve.wait_ge(sm.h["cTL"], tl)
            if k.ostr_guard[pscnt % 2]:
                ve.wait_ge(sm.h[f"dO{pscnt % 2}"], k.ostr_guard[pscnt % 2])
            sl = slice(ob * 512, (ob + 1) * 512)
            ve.tensor_scalar(k.af32[:], ps[:, 0:512], k.asw4[:, j:j + 1],
                             None, ALU.mult)
            ve.tensor_mul(k.af32[:], k.af32[:], k.gb_full[:, sl])
            ps_guard[pscnt % 2] = sm.inc(
                ve.tensor_add(k.ostr[pscnt % 2][:], k.af32[:],
                              k.xnew[:, j * DM + ob * 512:
                                     j * DM + (ob + 1) * 512]), "cVE")
            se.wait_ge(sm.h["cVE"], ps_guard[pscnt % 2])
            k.ostr_guard[pscnt % 2] = sm.dma(se.dma_start(
                out=k.d_y[(il * 4 + j) * 128:(il * 4 + j + 1) * 128, sl],
                in_=k.ostr[pscnt % 2][:]), f"dO{pscnt % 2}")
            pscnt += 1
    k.tailps_guard = list(ps_guard)


def emit_body2(k, half):
    sm = k.sm
    k.av_prev = 0
    k.ep_prev = 0
    k.ce_this = 0
    k.av_last = 0
    k.osb_done = 0
    k.ps_guard = [0, 0]
    k.ab_guard = [0, 0]
    k.kbuf_guard = [0, 0]
    k.psTr_guard = [0, 0]
    k.tailps_guard = [0, 0]
    k.ostr_guard = [0, 0]
    k.wbuf_guard = 0
    k.sqa_guard = 0
    k.xq_tr_last = 0
    for eng in [k.te, k.ve, k.se, k.sy, k.gp]:
        eng.wait_ge(sm.h["cGP"], k.negm_done)
        for n, v in k.spill_sems:
            eng.wait_ge(sm.h[n], v)
    for il, sb in enumerate(OWN_SB[half]):
        if 2 * il >= P2LIM:
            break
        emit_attention_sb(k, il, sb)
        if 2 * il + 1 >= P2LIM:
            break
        emit_tail_sb(k, il, sb)
    k.sy.wait_ge(sm.h["dO0"], sm.val("dO0"))
    k.sy.wait_ge(sm.h["dO1"], sm.val("dO1"))
    k.sy.wait_ge(sm.h["dO"], sm.val("dO"))


def build_nc():
    nc = bass.Bass("TRN2", target_bir_lowering=False, debug=False)
    est = ExitStack()
    k = K()
    k.nc = nc
    sm = Sems()
    k.sm = sm

    hx = nc.dram_tensor("x", [S, DM], F32, kind="ExternalInput")
    hrole = nc.dram_tensor("role", [1, 1], I32, kind="ExternalInput")
    hcT = nc.dram_tensor("cT", [1, DM], F32, kind="ExternalInput")
    hwmod = nc.dram_tensor("wmod", [DM, 6 * DM], F32, kind="ExternalInput")
    hwqT = nc.dram_tensor("wqT", [DM, 2 * DM], F32, kind="ExternalInput")
    hwkT = nc.dram_tensor("wkT", [DM, 2 * DM], F32, kind="ExternalInput")
    hwvT = nc.dram_tensor("wvT", [DM, DM], F32, kind="ExternalInput")
    hwoT = nc.dram_tensor("woT", [DM, DM], F32, kind="ExternalInput")
    hwfT = nc.dram_tensor("wfT", [DM, DM], F32, kind="ExternalInput")
    hlva = nc.dram_tensor("lva", [1, 16], F32, kind="ExternalInput")
    hlvb = nc.dram_tensor("lvb", [1, 16], F32, kind="ExternalInput")
    htri = nc.dram_tensor("tri", [128, 128], BF16, kind="ExternalInput")
    hid = nc.dram_tensor("ident", [128, 128], BF16, kind="ExternalInput")
    hy = nc.dram_tensor("y", [S // 2, DM], F32, kind="ExternalOutput")
    if KDBG:
        k.d_dbg_vec = nc.dram_tensor("dbg_vec", [128, 128 + 4 * DM], F32,
                                     kind="ExternalOutput").ap()
        k.d_dbg_hT = nc.dram_tensor("dbg_hT", [DM, S], BF16,
                                    kind="ExternalOutput").ap()
        k.d_dbg_qT = nc.dram_tensor("dbg_qT", [2 * DM, S // 2], BF16,
                                    kind="ExternalOutput").ap()
        k.d_dbg_kT = nc.dram_tensor("dbg_kT", [2 * DM, S], BF16,
                                    kind="ExternalOutput").ap()
        k.d_dbg_vS = nc.dram_tensor("dbg_vS", [S, DM], BF16,
                                    kind="ExternalOutput").ap()
        k.d_dbg_wfq = nc.dram_tensor("dbg_wfq", [DM, DM], BF16,
                                     kind="ExternalOutput").ap()
    hhT = nc.dram_tensor("hT_s", [DM, S], BF16, kind="Internal")
    hqT = nc.dram_tensor("qT_s", [2 * DM, S // 2], BF16, kind="Internal")
    hkT = nc.dram_tensor("kT_s", [2 * DM, S], BF16, kind="Internal")
    hvS = nc.dram_tensor("v_s", [S, DM], BF16, kind="Internal")
    hwfq = nc.dram_tensor("wfq_s", [DM, DM], BF16, kind="Internal")

    k.d_x = hx.ap()
    k.d_role = hrole.ap()
    k.d_lva, k.d_lvb = hlva.ap(), hlvb.ap()
    k.d_tri, k.d_id = htri.ap(), hid.ap()
    k.d_y = hy.ap()
    k.d_cT = hcT.ap()
    k.v_x = k.d_x.rearrange("(t p) d -> p t d", p=128)
    k.v_wmod = hwmod.ap().rearrange("(c p) o -> p c o", p=128)
    k.v_wqT = hwqT.ap().rearrange("(c p) o -> p c o", p=128)
    k.v_wkT = hwkT.ap().rearrange("(c p) o -> p c o", p=128)
    k.v_wvT = hwvT.ap().rearrange("(c p) o -> p c o", p=128)
    k.v_woT = hwoT.ap().rearrange("(c p) o -> p c o", p=128)
    k.v_wfT = hwfT.ap().rearrange("(c p) o -> p c o", p=128)
    k.v_hTw = hhT.ap().rearrange("(c p) t -> p c t", p=128)
    k.v_qT = hqT.ap().rearrange("(c p) t -> p c t", p=128)
    k.v_kT = hkT.ap().rearrange("(c p) t -> p c t", p=128)
    k.v_vS = hvS.ap().rearrange("(t p) d -> p t d", p=128)
    k.v_vSr = hvS.ap().rearrange("(c p) d -> p c d", p=128)
    k.v_wfq = hwfq.ap().rearrange("(c p) o -> p c o", p=128)
    k.hT_all, k.qT_all, k.kT_all = hhT.ap(), hqT.ap(), hkT.ap()
    k.vS_all, k.wfq_all = hvS.ap(), hwfq.ap()

    for n in ["dC", "dM0", "dM1", "dX0", "dX1", "dW", "dF", "dK0", "dK1",
              "dV", "dQ", "dXT0", "dXT1", "dWW", "dO", "dO0", "dO1",
              "dH0", "dH1", "dS0", "dS1", "dS2", "dS3", "dSH0", "dSH1",
              "dSV0", "dSV1", "dSQ",
              "cSE", "cVE", "cTE", "cEV", "cPJ", "cNM", "cQN",
              "cSC", "cAX", "cMK", "cAV", "cCE", "cEP", "cTL",
              "cQ2", "cGP"]:
        sm.add(n, est.enter_context(nc.semaphore(n)))

    k.ve_raw = nc.vector
    k.te, k.ve, k.se, k.gp, k.sy = (nc.tensor, VEGuard(nc.vector), nc.scalar,
                                    nc.gpsimd, nc.sync)

    def sbuf(name, shape, dt):
        return est.enter_context(nc.sbuf_tensor(name, shape, dt))

    def psum(name, shape, dt):
        return est.enter_context(nc.psum_tensor(name, shape, dt))

    k.psS0 = psum("psS0", [128, 512], F32)
    k.psS1 = psum("psS1", [128, 512], F32)
    k.psC = [psum(f"psC{j}", [128, 512], F32) for j in range(4)]
    k.psTr = psum("psTr", [128, 2048], BF16)

    k.rs = sbuf("rs", [1, 1], I32)
    k.ct = sbuf("ct", [128, 16], F32)
    k.ctb = sbuf("ctb", [128, 16], BF16)
    k.lva = sbuf("lva_s", [1, 16], F32)
    k.lvb = sbuf("lvb_s", [1, 16], F32)
    k.trim = sbuf("trim", [128, 128], BF16)
    k.identb = sbuf("identb", [128, 128], BF16)
    k.ones_b = sbuf("ones_b", [128, 1], BF16)
    k.onesf = sbuf("onesf", [128, 1], F32)
    k.modT = sbuf("modT", [128, 32], F32)
    k.ca1v = sbuf("ca1v", [128, 16], F32)
    k.ga_full = sbuf("ga_full", [128, DM], F32)
    k.gb_full = sbuf("gb_full", [128, DM], F32)
    k.cb1_full = sbuf("cb1_full", [128, DM], F32)
    k.sb_full = sbuf("sb_full", [128, DM], F32)
    k.lam_pb = sbuf("lam_pb", [128, 1], F32)
    k.ws_pb = sbuf("ws_pb", [128, 1], F32)
    k.rws_pb = sbuf("rws_pb", [128, 1], F32)
    k.negm_pb = sbuf("negm_pb", [128, 16], F32)
    k.maxq2 = sbuf("maxq2", [1, 16], F32)
    k.maxk2 = sbuf("maxk2", [1, 16], F32)
    k.tiny = sbuf("tiny", [1, 16], F32)
    k.tiny2 = sbuf("tiny2", [1, 16], F32)
    k.wsum = sbuf("wsum", [128, 1], F32)
    k.eps5 = sbuf("eps5", [128, 1], F32)
    k.eps6 = sbuf("eps6", [128, 1], F32)
    k.onesr = sbuf("onesr", [1, 128], F32)
    if KDBG:
        k.dbgst = sbuf("dbgst", [128, 12], F32)
        k.dbgstage = sbuf("dbgstage", [128, 128], F32)
    k.bc_cnt = 0
    k.bc_hist = [0, 0]
    k.bc_te_last = 0
    k.wfred = sbuf("wfred", [128, 1], F32)

    with nc.Block() as block:
        @block.gpsimd
        def _(gph):
            emit_consts(k)
            if PHASE >= 1:
                with ExitStack() as p0:
                    emit_p0(k, p0)
            with ExitStack() as p1:
                def sb1(name, shape, dt):
                    return p1.enter_context(nc.sbuf_tensor(name, shape, dt))
                if PHASE >= 2:
                    emit_p1a(k, p1)
                k.wres = sb1("wres", [128, 16 * 2048], BF16)
                k.htsl = [sb1(f"htsl{i}", [128, 16 * 512], BF16)
                          for i in range(2)]
                k.stg = [sb1(f"stg{i}", [128, 512], BF16) for i in range(4)]
                k.sqt = [sb1(f"sqt{i}", [128, 512], BF16) for i in range(2)]
                k.vts = [sb1(f"vts{i}", [128, 512], BF16) for i in range(2)]
                k.wfbuf2 = sb1("wfbuf2", [128, 1024], F32)
                k.qv = sb1("qv", [128, 512], F32)
                k.qi = sb1("qi", [128, 512], I32)
                k.qf = sb1("qf", [128, 512], F32)
                k.qe = sb1("qe", [128, 512], F32)
                k.qt1 = sb1("qt1", [128, 512], F32)
                k.wqstage = sb1("wqstage", [128, DM], BF16)

                kblocks = [(nb, nb) for nb in range(8)]
                if PHASE >= 3:
                    emit_proj_pass(k, k.v_wkT, 0, 16, kblocks, k.v_kT, k.maxk2)
                    emit_proj_pass(k, k.v_wkT, 16, 32, kblocks, k.v_kT, k.maxk2)
                if PHASE >= 4:
                    emit_v_pass(k)
                if PHASE >= 5:
                    emit_wfq(k)
                if PHASE >= 6:
                    snap = sm.snap()
                    ends = []
                    for i in nc.Switch(engines=k.sw_engines, index=k.idx, n=2):
                        sm.restore(snap)
                        own = OWN_SB[i]
                        qblocks = [(own[t], t) for t in range(4)]
                        emit_proj_pass(k, k.v_wqT, 0, 16, qblocks, k.v_qT,
                                       k.maxq2)
                        emit_proj_pass(k, k.v_wqT, 16, 32, qblocks, k.v_qT,
                                       k.maxq2)
                        ends.append(sm.snap())
                    assert ends[0] == ends[1], "switch1 asymmetric"
                    emit_negm(k)
                k.spill_sems = [(n, sm.val(n)) for n in
                                ("dS0", "dS1", "dS2", "dS3", "dSH0", "dSH1",
                                 "dSV0", "dSV1", "dSQ")]
                if KDBG:
                    emit_debug_dumps(k)

            with ExitStack() as p2:
                def sb2(name, shape, dt):
                    return p2.enter_context(nc.sbuf_tensor(name, shape, dt))
                k.kbuf = [sb2(f"kbuf{i}", [128, 2 * 1024], BF16)
                          for i in range(2)]
                k.vsb = sb2("vsb", [128, 32 * 257], BF16)
                k.qslab = sb2("qslab", [128, 2048], BF16)
                k.abuf = [sb2(f"abuf{i}", [128, 512], BF16) for i in range(2)]
                k.Csb = sb2("Csb", [128, 4 * 257], F32)
                k.rc0 = sb2("rc0", [128, 1], F32)
                k.rc1 = sb2("rc1", [128, 1], F32)
                k.t1f = sb2("t1f", [128, 256], F32)
                k.off = sb2("off", [128, 256], F32)
                k.sqf = sb2("sqf", [128, 256], F32)
                k.grp8 = sb2("grp8", [128, 8], F32)
                k.grs = sb2("grs", [128, 8], F32)
                k.grr = sb2("grr", [128, 8], F32)
                k.o_sb = sb2("o_sb", [128, 4 * DM], BF16)
                k.oT_sb = sb2("oT_sb", [128, 16 * 512], BF16)
                k.xnew = sb2("xnew", [128, 4 * DM], F32)
                k.xstr = [sb2(f"xstr{i}", [128, 512], F32) for i in range(2)]
                k.wbuf = sb2("wbuf", [128, 16 * 512], BF16)
                k.af32 = sb2("af32", [128, 512], F32)
                k.xqT = sb2("xqT", [128, 4 * DM], BF16)
                k.xqtok = sb2("xqtok", [128, DM], BF16)
                k.h2 = sb2("h2", [128, DM], F32)
                k.s2_sum = sb2("s2_sum", [128, 1], F32)
                k.s2_mu = sb2("s2_mu", [128, 1], F32)
                k.s2_mu2 = sb2("s2_mu2", [128, 1], F32)
                k.s2_sqa = sb2("s2_sqa", [128, 4], F32)
                k.s2_ssq = sb2("s2_ssq", [128, 1], F32)
                k.s2_var = sb2("s2_var", [128, 1], F32)
                k.s2_sd = sb2("s2_sd", [128, 1], F32)
                k.s2_rstd = sb2("s2_rstd", [128, 1], F32)
                k.s2_am4 = sb2("s2_am4", [128, 4], F32)
                k.s2_amax = sb2("s2_amax", [128, 1], F32)
                k.s2_ra = sb2("s2_ra", [128, 1], F32)
                k.s2_r = sb2("s2_r", [128, 1], F32)
                k.s2_ar = sb2("s2_ar", [128, 1], F32)
                k.asw4 = sb2("asw4", [128, 4], F32)
                k.ostr = [sb2(f"ostr{i}", [128, 512], F32) for i in range(2)]
                k.qv = sb2("qv2", [128, 512], F32)
                k.qi = sb2("qi2", [128, 512], I32)
                k.qf = sb2("qf2", [128, 512], F32)
                k.qe = sb2("qe2", [128, 512], F32)
                k.qt1 = sb2("qt12", [128, 512], F32)

                for n, v in k.spill_sems:
                    k.ve_raw.wait_ge(sm.h[n], v)
                ins = None
                for kc in range(32):
                    ins = k.ve.memset(k.vsb[:, kc * 257 + 256:kc * 257 + 257],
                                      1.0)
                k.vsb_ones = sm.inc(ins, "cVE")

                if PHASE >= 7:
                    snap = sm.snap()
                    for i in nc.Switch(engines=k.sw_engines, index=k.idx, n=2):
                        sm.restore(snap)
                        emit_body2(k, i)

    est.close()
    return nc


_NC_CACHE = {}


def _get_nc():
    if "nc" not in _NC_CACHE:
        _NC_CACHE["nc"] = build_nc()
    return _NC_CACHE["nc"]


def kernel(L0_attn_x, L0_attn_l_q1, L0_attn_l_k1, L0_attn_l_q2, L0_attn_l_k2,
           L0_attn_wq, L0_attn_wk, L0_attn_wv, L0_attn_wo, L0_ffn_weight,
           L0_c, L0_wmod):
    nc = _get_nc()
    f32 = np.float32
    bf16 = ml_dtypes.bfloat16
    wqT = np.ascontiguousarray(np.asarray(L0_attn_wq, f32).T)
    wkT = np.ascontiguousarray(np.asarray(L0_attn_wk, f32).T)
    wvT = np.ascontiguousarray(np.asarray(L0_attn_wv, f32).T)
    woT = np.ascontiguousarray(np.asarray(L0_attn_wo, f32).T)
    wfT = np.ascontiguousarray(np.asarray(L0_ffn_weight, f32).T)
    wmod = np.ascontiguousarray(np.asarray(L0_wmod, f32))
    lva = np.ascontiguousarray(np.concatenate(
        [np.asarray(L0_attn_l_q1, f32), np.asarray(L0_attn_l_q2, f32)])[None, :])
    lvb = np.ascontiguousarray(np.concatenate(
        [np.asarray(L0_attn_l_k1, f32), np.asarray(L0_attn_l_k2, f32)])[None, :])
    tri = np.triu(np.ones((128, 128), dtype=bf16))
    ident = np.eye(128, dtype=bf16)
    x = np.asarray(L0_attn_x, f32)
    cmat = np.asarray(L0_c, f32)

    in_maps = []
    for c in range(8):
        b, half = c // 2, c % 2
        in_maps.append({
            "x": np.ascontiguousarray(x[b]),
            "role": np.array([[half]], np.int32),
            "cT": np.ascontiguousarray(cmat[b][None, :]),
            "wmod": wmod,
            "wqT": wqT, "wkT": wkT, "wvT": wvT, "woT": woT, "wfT": wfT,
            "lva": lva, "lvb": lvb, "tri": tri, "ident": ident,
        })
    res = run_bass_kernel_spmd(nc, in_maps, core_ids=list(range(8)))
    y = np.empty((4, S, DM), np.float32)
    for c in range(8):
        b, half = c // 2, c % 2
        yc = np.asarray(res.results[c]["y"])
        for i, sb in enumerate(OWN_SB[half]):
            y[b, sb * 512:(sb + 1) * 512] = yc[i * 512:(i + 1) * 512]
    return y
